# version 01
# baseline (speedup 1.0000x reference)
import numpy as np
import ml_dtypes
from contextlib import ExitStack

import concourse.bass as bass
import concourse.mybir as mybir
from concourse.bass_utils import run_bass_kernel_spmd

F32 = mybir.dt.float32
BF16 = mybir.dt.bfloat16
I32 = mybir.dt.int32
AF = mybir.ActivationFunctionType
ALU = mybir.AluOpType
AX = mybir.AxisListType
NPBF16 = ml_dtypes.bfloat16

D = 2048
NTOK = 2048
SEQ = 4096
NCORES = 8

BASE_LR = 0.01
BASE_LR_INV = float(BASE_LR + np.log(-np.expm1(-BASE_LR)))
NS_COEFFS = [(4.0848, -6.8946, 2.927), (3.9505, -6.3029, 2.6377),
             (3.7418, -5.5913, 2.3037), (2.8769, -3.1427, 1.2046),
             (2.8366, -3.0525, 1.2012)]

ENGINES = ("sync", "gpsimd", "vector", "scalar", "tensor")
PARANOID = False


class Prog:

    def __init__(self, nc, stack):
        self.nc = nc
        self.stack = stack
        self.items = {e: [] for e in ENGINES}
        self.sems = {}
        self.cnt = {}
        self._names = set()
        self._guards = {}
        self._waited = {}
        self._consts = {}
        self._semowner = {}

    def const(self, value, dtype=F32):
        key = (dtype, float(value))
        if key not in self._consts:
            ap = self.sbuf(f"const{len(self._consts)}", [128, 1], dtype)
            v = self.emit('gpsimd',
                          lambda g, ap=ap, value=value: g.memset(ap[:, :], value),
                          sem="s_const")
            for e in ("scalar", "vector", "tensor"):
                self.wait(e, "s_const", v)
            self.nc.const_aps.register(dtype, float(value), ap[:, 0:1])
            self._consts[key] = ap
        return self._consts[key][:, 0:1]

    def sem(self, name):
        if name not in self.sems:
            self.sems[name] = self.stack.enter_context(self.nc.semaphore(name))
            self.cnt[name] = 0
        return self.sems[name]

    def sbuf(self, name, shape, dtype):
        assert name not in self._names, name
        self._names.add(name)
        return self.stack.enter_context(self.nc.sbuf_tensor(name, shape, dtype))

    def psum(self, name, shape, dtype):
        assert name not in self._names, name
        self._names.add(name)
        return self.stack.enter_context(self.nc.psum_tensor(name, shape, dtype))

    def on(self, engine, fn):
        self.items[engine].append(fn)

    def wait(self, engine, semname, val):
        if val <= 0:
            return
        if self._waited.get((engine, semname), -1) >= val:
            return
        self._waited[(engine, semname)] = val
        s = self.sem(semname)
        self.on(engine, lambda eng, s=s, v=val: eng.wait_ge(s, v))

    def wait_cur(self, engine, semname):
        self.sem(semname)
        self.wait(engine, semname, self.cnt[semname])

    def inc(self, semname, by, engine=None):
        self.sem(semname)
        if engine is not None and not semname.startswith("s_sp"):
            owner = self._semowner.setdefault(semname, engine)
            assert owner == engine, (
                f"semaphore {semname} inc'd from {engine}, owned by {owner}: "
                f"multi-engine increments make waits unreliable")
        self.cnt[semname] += by
        return self.cnt[semname]

    def emit(self, engine, fn, sem=None, by=1, drain_inc=False):
        if sem is None:
            self.on(engine, fn)
            return None
        s = self.sem(sem)
        v = self.inc(sem, by, engine=engine)
        if drain_inc:
            def wrapped(eng, fn=fn, s=s, by=by):
                fn(eng)
                return eng.drain().then_inc(s, by)
            self.on(engine, wrapped)
        else:
            self.on(engine, lambda eng, fn=fn, s=s, by=by: fn(eng).then_inc(s, by))
        return v

    def dma(self, engine, out, in_, sem=None, **kw):
        return self.emit(
            engine,
            lambda eng, o=out, i=in_, kw=dict(kw): eng.dma_start(out=o, in_=i, **kw),
            sem=sem, by=16)

    def hop(self, engine, semname, val):
        self.wait(engine, semname, val)
        hs = f"s_hp_{engine}"
        return hs, self.emit(engine, lambda eng: eng.drain(), sem=hs)

    def sync_point(self, engines=("tensor", "vector", "scalar", "gpsimd")):
        vals = {}
        for e in engines:
            vals[e] = self.emit(e, lambda eng: eng.drain(), sem="s_sp")
        tgt = self.cnt["s_sp"]
        for e in engines:
            self.wait(e, "s_sp", tgt)

    def guard_set(self, resource, semname, val):
        d = self._guards.setdefault(resource, {})
        d[semname] = max(d.get(semname, 0), val)

    def guard(self, engine, resource):
        g = self._guards.get(resource)
        if g is not None:
            for semname, val in g.items():
                self.wait(engine, semname, val)

    def replay(self):
        with self.nc.Block() as block:
            for e in ENGINES:
                if self.items[e]:
                    def mk(lst):
                        def run(eng):
                            for fn in lst:
                                fn(eng)
                        return run
                    getattr(block, e)(mk(self.items[e]))


def acc_group(p, psum_ap, ops, sem=None, by=1):
    ops = list(ops)

    def fn(t, ops=ops, ps=psum_ap):
        last = None
        n = len(ops)
        for i, (a, b) in enumerate(ops):
            last = t.matmul(ps, a, b, start=(i == 0), stop=(i == n - 1))
        return last
    return p.emit('tensor', fn, sem=sem, by=by)


def emit_identity(p, ident):
    def mkid(g, idn=ident):
        g.memset(idn[:, :], 0.0)
        g.drain()
        return g.affine_select(out=idn[:, :], in_=idn[:, :],
                               compare_op=ALU.not_equal, fill=1.0, base=0,
                               pattern=[[-1, 128]], channel_multiplier=1)
    v = p.emit('gpsimd', mkid, sem="s_id")
    p.wait('tensor', "s_id", v)



def prep_T(w):
    out_d, in_d = w.shape
    assert in_d % 128 == 0
    wT = np.ascontiguousarray(np.asarray(w).T)
    return np.ascontiguousarray(wT.reshape(in_d // 128, 128, out_d))


def prep_T_lhs(w):
    out_d, in_d = np.asarray(w).shape
    assert out_d % 128 == 0 and in_d % 128 == 0
    wT = np.asarray(w).T
    arr = wT.reshape(in_d // 128, 128, out_d // 128, 128)
    return np.ascontiguousarray(
        arr.transpose(2, 1, 0, 3).reshape(out_d // 128, 128, in_d))



def emit_ln_tile(p, xtile, xln_bf, scratch, mean_ap, rstd_ap, sem, width=D):
    inv = 1.0 / width

    semS = sem + "S"

    def v1(v, xt=xtile, m=mean_ap):
        v.tensor_reduce(out=m, in_=xt, axis=AX.X, op=ALU.add)
        v.drain()
        return v.tensor_scalar_mul(m, m, inv)
    p.emit('vector', v1, sem=sem)

    def v2(v, xt=xtile, m=mean_ap, sc=scratch):
        v.drain()
        return v.tensor_scalar(out=sc, in0=xt, scalar1=m, scalar2=None,
                               op0=ALU.subtract)
    p.emit('vector', v2, sem=sem)
    p.wait_cur('scalar', sem)

    def s1(s, sc=scratch, r=rstd_ap):
        return s.activation(out=sc, in_=sc, func=AF.Square, accum_out=r)
    p.emit('scalar', s1, sem=semS)

    def s2(s, r=rstd_ap):
        s.drain()
        return s.activation(out=r, in_=r, func=AF.Sqrt, scale=inv, bias=1e-5)
    p.emit('scalar', s2, sem=semS)
    p.wait_cur('vector', semS)

    def v3(v, r=rstd_ap):
        return v.reciprocal(out=r, in_=r)
    p.emit('vector', v3, sem=sem)

    def v4(v, xt=xtile, xb=xln_bf, m=mean_ap, r=rstd_ap):
        v.drain()
        return v.tensor_scalar(out=xb, in0=xt, scalar1=m, scalar2=r,
                               op0=ALU.subtract, op1=ALU.mult)
    return p.emit('vector', v4, sem=sem)


def emit_transpose_tile(p, src_bf, dst_cols, pt_psum, ident, sem_pe, sem_ev,
                        ngrp=16, evac='scalar'):
    nround = (ngrp + 7) // 8
    vev = None
    for r in range(nround):
        blocks = range(r * 8, min(ngrp, r * 8 + 8))
        p.guard('tensor', id(pt_psum))

        def tf(t, src=src_bf, ps=pt_psum, idn=ident, blocks=list(blocks)):
            last = None
            for i, g in enumerate(blocks):
                last = t.transpose(ps[:, i * 128:(i + 1) * 128],
                                   src[:, g * 128:(g + 1) * 128], idn)
            return last
        vpe = p.emit('tensor', tf, sem=sem_pe)
        p.wait(evac, sem_pe, vpe)

        def ev(eng, ps=pt_psum, blocks=list(blocks), dc=dst_cols, evac=evac):
            last = None
            for i, g in enumerate(blocks):
                if evac == 'scalar':
                    last = eng.copy(out=dc(g), in_=ps[:, i * 128:(i + 1) * 128])
                else:
                    last = eng.tensor_copy(out=dc(g), in_=ps[:, i * 128:(i + 1) * 128])
            return last
        vev = p.emit(evac, ev, sem=sem_ev)
        other = 'vector' if evac == 'scalar' else 'scalar'
        hv = p.hop(other, sem_ev, vev)
        p.guard_set(id(pt_psum), *hv)
    return vev



NT = 512
NSUB = NTOK // NT
KD = D // 128
KI = 1408 // 128
TT = NT // 128


def mk_shared_bufs(p, sb, big=True, psums=True):
    if "ident" not in sb:
        sb["ident"] = p.sbuf("c_ident", [128, 128], BF16)
        emit_identity(p, sb["ident"])
        if big:
            sb["xt"] = [p.sbuf(f"c_xt{i}", [128, D], F32) for i in range(2)]
            sb["scr"] = p.sbuf("c_scr", [128, D], F32)
            sb["xlnbf"] = p.sbuf("c_xlnbf", [128, D], BF16)
        if big:
            sb["xlnT"] = p.sbuf("c_xlnT", [128, KD * NT], BF16)
        sb["mean"] = p.sbuf("c_mean", [128, 4 * TT], F32)
        sb["rstd"] = p.sbuf("c_rstd", [128, 4 * TT], F32)
        sb["mean2"] = p.sbuf("c_mean2", [128, 4 * TT], F32)
        sb["rstd2"] = p.sbuf("c_rstd2", [128, 4 * TT], F32)
        if big:
            sb["acc"] = p.sbuf("c_acc", [128, TT * D], F32)
            sb["u1"] = p.sbuf("c_u1", [128, 32768], BF16)
            sb["hbf"] = sb["u1"][:, KI * D:KI * D + KI * NT]
            sb["wdb"] = sb["u1"][:, 0:KI * D]
        sb["wgb"] = [p.sbuf(f"c_wgb{i}", [128, D], BF16) for i in range(2)]
        if big:
            sb["wub"] = [p.sbuf(f"c_wub{i}", [128, D], BF16) for i in range(2)]
        sb["rwb"] = p.sbuf("c_rwb", [128, KD * 8], BF16)
        if big:
            sb["comb"] = p.sbuf("c_comb", [128, TT * 8], F32)
            sb["rtscr"] = p.sbuf("c_rtscr", [128, 8], F32)
            sb["rstat"] = p.sbuf("c_rstat", [128, 4 * TT], F32)
            sb["sgev"] = p.sbuf("c_sgev", [128, NT], F32)
        if big:
            sb["lnstage"] = p.sbuf("c_lnstage", [128, KD * NT], BF16)
        sb["hr"] = p.sbuf("c_hr", [128, 2], F32)
        if psums:
            sb["pg"] = [p.psum(f"c_pg{i}", [128, NT], F32) for i in range(2)]
            sb["pu"] = [p.psum(f"c_pu{i}", [128, NT], F32) for i in range(2)]
            sb["pd"] = [p.psum(f"c_pd{i}", [128, 512], F32) for i in range(2)]
            sb["pt"] = p.psum("c_pt", [128, 1024], BF16)
            sb["prt"] = p.psum("c_prt", [128, 512], F32)


def build_moe_sublayer(p, tag, xin_ap, x1_ap, lnoutT_ap, w, shared_bufs, dbg=None):
    sb = shared_bufs
    mk_shared_bufs(p, sb)
    ident = sb["ident"]; xt = sb["xt"]; scr = sb["scr"]; xlnbf = sb["xlnbf"]
    xlnT = sb["xlnT"]; acc = sb["acc"]; hbf = sb["hbf"]
    wgb = sb["wgb"]; wub = sb["wub"]; wdb = sb["wdb"]; rwb = sb["rwb"]
    comb = sb["comb"]; rtscr = sb["rtscr"]; rstat = sb["rstat"]
    sgev = sb["sgev"]; lnstage = sb["lnstage"]; hr = sb["hr"]
    pg = sb["pg"]; pu = sb["pu"]; pd = sb["pd"]; pt = sb["pt"]; prt = sb["prt"]
    mean, rstd = sb["mean"], sb["rstd"]
    mean2, rstd2 = sb["mean2"], sb["rstd2"]

    p.guard('gpsimd', id(rwb))
    vrw = p.dma('gpsimd', rwb[:, :], w["rwT"][:, :], sem="s_wdma")

    for s in range(NSUB):
        tok0 = s * NT
        for t in range(TT):
            buf = xt[t % 2]
            p.guard('sync', id(buf))
            p.guard('sync', (id(buf), 'st'))
            sx = f"s_x{t % 2}"
            vx = p.dma('sync', buf[:, :],
                       xin_ap[tok0 + t * 128: tok0 + (t + 1) * 128, :],
                       sem=sx)
            p.wait('vector', sx, vx)
            p.wait('scalar', sx, vx)
            p.guard('scalar', id(xlnT))
            p.guard('vector', id(xlnbf))
            vln = emit_ln_tile(p, buf[:, :], xlnbf[:, :], scr[:, :],
                               mean[:, 4 * t:4 * t + 1], rstd[:, 4 * t:4 * t + 1], "s_ln")
            p.guard_set(id(buf), "s_ln", vln)
            p.wait_cur('tensor', "s_ln")
            vte = emit_transpose_tile(
                p, xlnbf,
                (lambda g, t=t: xlnT[:, g * NT + t * 128: g * NT + (t + 1) * 128]),
                pt, ident[:, :], "s_tp", "s_te")
            p.guard_set(id(xlnbf), "s_tp", p.cnt["s_tp"])
        if PARANOID:
            p.sync_point()
        p.wait_cur('tensor', "s_te")
        p.wait('tensor', "s_wdma", vrw)
        for t in range(TT):
            p.guard('tensor', id(prt))
            vmm = acc_group(p, prt[:, 0:8],
                            [(xlnT[:, g * NT + t * 128: g * NT + (t + 1) * 128],
                              rwb[:, g * 8:(g + 1) * 8]) for g in range(KD)],
                            sem="s_rt")
            p.wait('vector', "s_rt", vmm)
            c8 = comb[:, t * 8:(t + 1) * 8]

            rs = rstat[:, 4 * t:4 * t + 4]
            def vr(v, lg=prt[:, 0:8], c8=c8, rs=rs):
                v.reduce_max(out=rs[:, 0:1], in_=lg, axis=AX.X, negate=True)
                v.drain()
                return v.tensor_scalar(out=c8, in0=lg, scalar1=rs[:, 0:1],
                                       scalar2=None, op0=ALU.add)
            vpr = p.emit('vector', vr, sem="s_rtv", drain_inc=True)
            p.wait('scalar', "s_rtv", vpr)

            def sr(s_, c8=c8, rs=rs):
                return s_.activation(out=c8, in_=c8, func=AF.Exp,
                                     accum_out=rs[:, 1:2])
            vsr = p.emit('scalar', sr, sem="s_rts", drain_inc=True)
            p.wait('vector', "s_rts", vsr)

            def vr2(v, c8=c8, rs=rs, tsh=rtscr):
                ts = tsh[:, :]
                v.reciprocal(out=rs[:, 1:2], in_=rs[:, 1:2])
                v.drain()
                v.tensor_scalar_mul(c8, c8, rs[:, 1:2])
                v.drain()
                v.reduce_max(out=rs[:, 2:3], in_=c8, axis=AX.X)
                v.drain()
                v.tensor_scalar(out=ts, in0=c8, scalar1=rs[:, 2:3], scalar2=None,
                                op0=ALU.is_ge)
                v.drain()
                v.scalar_tensor_tensor(out=ts, in0=ts, scalar=-2.0, in1=c8,
                                       op0=ALU.mult, op1=ALU.add)
                v.drain()
                v.reduce_max(out=rs[:, 3:4], in_=ts, axis=AX.X)
                v.drain()
                v.tensor_scalar(out=ts, in0=c8, scalar1=rs[:, 3:4], scalar2=None,
                                op0=ALU.is_ge)
                v.drain()
                return v.tensor_mul(c8, c8, ts)
            vlast = p.emit('vector', vr2, sem="s_rtv")
            p.guard_set(id(prt), "s_rtv", vlast)
        if dbg is not None and s == 0:
            p.wait_cur('sync', "s_rt")
            for t in range(TT):
                p.dma('sync', dbg["comb2"][t * 128:(t + 1) * 128, :],
                      comb[:, t * 8:(t + 1) * 8], sem="s_dbg")
            vdbg = p.cnt["s_dbg"]
            p.wait('tensor', "s_dbg", vdbg)
        if PARANOID:
            p.sync_point()
        for e in range(9):
            shared = (e == 8)
            src_g = w["sgT"] if shared else w["wgT"][e]
            src_u = w["suT"] if shared else w["wuT"][e]
            src_d = w["sdT"] if shared else w["wdT"][e]
            for j in range(KI):
                par = j % 2
                gb, ub = wgb[par], wub[par]
                pgj, puj = pg[par], pu[par]
                p.guard('gpsimd', id(gb))
                vg = p.dma('gpsimd', gb[:, :], src_g[j], sem=f"s_wg{par}")
                p.guard('gpsimd', id(ub))
                vu = p.dma('gpsimd', ub[:, :], src_u[j], sem=f"s_wu{par}")
                p.wait('tensor', f"s_wg{par}", vg)
                p.wait('tensor', f"s_wu{par}", vu)
                p.guard('tensor', id(pgj))
                vgm = acc_group(p, pgj[:, :],
                                [(gb[:, g * 128:(g + 1) * 128],
                                  xlnT[:, g * NT:(g + 1) * NT]) for g in range(KD)],
                                sem="s_gu")
                p.guard_set(id(gb), "s_gu", vgm)
                p.guard('tensor', id(puj))
                vum = acc_group(p, puj[:, :],
                                [(ub[:, g * 128:(g + 1) * 128],
                                  xlnT[:, g * NT:(g + 1) * NT]) for g in range(KD)],
                                sem="s_gu")
                p.guard_set(id(ub), "s_gu", vum)
                p.wait('scalar', "s_gu", vum)
                p.guard('scalar', id(sgev))
                p.guard('vector', "u1")
                vsg = p.emit('scalar', lambda sc, ps=pgj, sg=sgev: sc.activation(
                    out=sg[:, :], in_=ps[:, :], func=AF.Sigmoid), sem="s_hs")
                p.wait('vector', "s_hs", vsg)
                p.guard('vector', (id(hbf), j))
                vh = p.emit('vector', lambda v, sg=sgev, ps=puj, hb=hbf, j=j:
                            v.tensor_tensor(out=hb[:, j * NT:(j + 1) * NT],
                                            in0=sg[:, :], in1=ps[:, :], op=ALU.mult),
                            sem="s_h")
                hv = p.hop('scalar', "s_h", vh)
                p.guard_set(id(pgj), *hv)
                p.guard_set(id(puj), *hv)
                p.guard_set(id(sgev), *hv)
            p.guard('gpsimd', id(wdb))
            p.guard('gpsimd', "u1")
            vd = p.dma('gpsimd', wdb[:, :].rearrange("q (j d) -> q j d", j=KI),
                       src_d.rearrange("j q d -> q j d"), sem="s_wd")
            p.wait('tensor', "s_wd", vd)
            p.wait_cur('tensor', "s_h")
            lastdn = None
            for t in range(TT):
                for n in range(4):
                    ps = pd[(t * 4 + n) % 2]
                    p.guard('tensor', id(ps))
                    vdn = acc_group(
                        p, ps[:, :],
                        [(hbf[:, j * NT + t * 128: j * NT + (t + 1) * 128],
                          wdb[:, j * D + n * 512: j * D + (n + 1) * 512])
                         for j in range(KI)],
                        sem="s_dnmm")
                    p.wait('vector', "s_dnmm", vdn)
                    at = acc[:, t * D + n * 512: t * D + (n + 1) * 512]
                    if e == 0:
                        p.guard('vector', "u2acc")
                    if e == 0:
                        fn = (lambda v, ps=ps, at=at, c8=comb, t=t, e=e:
                              v.tensor_scalar(out=at, in0=ps[:, :],
                                              scalar1=c8[:, t * 8 + e: t * 8 + e + 1],
                                              scalar2=None, op0=ALU.mult))
                    elif shared:
                        fn = (lambda v, ps=ps, at=at:
                              v.tensor_add(at, at, ps[:, :]))
                    else:
                        fn = (lambda v, ps=ps, at=at, c8=comb, t=t, e=e:
                              v.scalar_tensor_tensor(
                                  out=at, in0=ps[:, :],
                                  scalar=c8[:, t * 8 + e: t * 8 + e + 1],
                                  in1=at, op0=ALU.mult, op1=ALU.add))
                    lastdn = p.emit('vector', fn, sem="s_dn")
                    hv = p.hop('scalar', "s_dn", lastdn)
                    p.guard_set(id(ps), *hv)
            for j in range(KI):
                p.guard_set((id(hbf), j), "s_dnmm", p.cnt["s_dnmm"])
            p.guard_set(id(wdb), "s_dnmm", p.cnt["s_dnmm"])
            if PARANOID:
                p.sync_point()
        if dbg is not None and s == 0:
            p.wait_cur('sync', "s_dn")
            p.wait_cur('sync', "s_rt")
            for t in range(TT):
                p.dma('sync', dbg["acc"][t * 128:(t + 1) * 128, :],
                      acc[:, t * D:(t + 1) * D], sem="s_dbg")
                p.dma('sync', dbg["comb"][t * 128:(t + 1) * 128, :],
                      comb[:, t * 8:(t + 1) * 8], sem="s_dbg")
            p.dma('sync', dbg["xlnT"].rearrange("(g q) t -> q g t", q=128),
                  xlnT[:, :].rearrange("q (g t) -> q g t", g=KD), sem="s_dbg")
            p.wait_cur('sync', "s_dbg")
        if PARANOID:
            p.sync_point()
        for t in range(TT):
            buf = xt[t % 2]
            p.guard('sync', id(buf))
            p.guard('sync', (id(buf), 'st'))
            sx = f"s_x{t % 2}"
            vx = p.dma('sync', buf[:, :],
                       xin_ap[tok0 + t * 128: tok0 + (t + 1) * 128, :],
                       sem=sx)
            p.wait('vector', sx, vx)
            p.wait_cur('vector', "s_dn")

            def vc0(v, m=mean, r=rstd, t=t, hr=hr):
                v.tensor_scalar_mul(hr[:, 0:1], r[:, 4 * t:4 * t + 1], 0.5)
                v.drain()
                return v.tensor_mul(hr[:, 1:2], hr[:, 0:1], m[:, 4 * t:4 * t + 1])
            p.emit('vector', vc0, sem="s_cb")

            def vc1(v, buf=buf, at=acc, t=t, hr=hr):
                v.drain()
                v.scalar_tensor_tensor(out=at[:, t * D:(t + 1) * D],
                                       in0=at[:, t * D:(t + 1) * D], scalar=0.5,
                                       in1=buf[:, :], op0=ALU.mult, op1=ALU.add)
                v.tensor_scalar(out=buf[:, :], in0=buf[:, :], scalar1=hr[:, 0:1],
                                scalar2=hr[:, 1:2], op0=ALU.mult, op1=ALU.subtract)
                v.drain()
                return v.tensor_add(buf[:, :], buf[:, :],
                                    at[:, t * D:(t + 1) * D])
            vcb = p.emit('vector', vc1, sem="s_cb")
            p.wait('sync', "s_cb", vcb)
            sst = f"s_st{t % 2}"
            vst = p.dma('sync', x1_ap[tok0 + t * 128: tok0 + (t + 1) * 128, :],
                        buf[:, :], sem=sst)
            p.guard_set((id(buf), 'st'), sst, vst)
            if lnoutT_ap is not None:
                p.wait('scalar', "s_cb", vcb)
                p.guard('scalar', id(lnstage))
                p.guard('vector', id(xlnbf))
                vln2 = emit_ln_tile(p, buf[:, :], xlnbf[:, :], scr[:, :],
                                    mean2[:, 4 * t:4 * t + 1], rstd2[:, 4 * t:4 * t + 1], "s_ln")
                p.guard_set(id(buf), "s_ln", vln2)
                p.wait_cur('tensor', "s_ln")
                emit_transpose_tile(
                    p, xlnbf,
                    (lambda g, t=t: lnstage[:, g * NT + t * 128: g * NT + (t + 1) * 128]),
                    pt, ident[:, :], "s_tp", "s_te")
                p.guard_set(id(xlnbf), "s_tp", p.cnt["s_tp"])
            else:
                p.guard_set(id(buf), "s_cb", vcb)
        if lnoutT_ap is not None:
            p.wait_cur('sync', "s_te")
            for g in range(KD):
                vls = p.dma('sync',
                            lnoutT_ap[g * 128:(g + 1) * 128, tok0:tok0 + NT],
                            lnstage[:, g * NT:(g + 1) * NT], sem="s_lnst")
            p.guard_set(id(lnstage), "s_lnst", p.cnt["s_lnst"])
        p.guard_set(id(xlnT), "s_gu", p.cnt["s_gu"])
        p.guard_set(id(rwb), "s_rt", p.cnt["s_rt"])
        p.guard_set(id(comb), "s_dn", p.cnt["s_dn"])
    p.guard_set("u1", "s_dnmm", p.cnt["s_dnmm"])
    p.guard_set("u1", "s_wd", p.cnt["s_wd"])
    p.guard_set("u1", "s_h", p.cnt["s_h"])
    p.guard_set("u2acc", "s_cb", p.cnt["s_cb"])
    p.guard_set("u2acc", "s_dn", p.cnt["s_dn"])



def moe_weight_aps(nc, pfx="m0"):
    k = "ExternalInput"
    w = {}
    w["sgT"] = nc.dram_tensor(f"{pfx}_sgT", [11, 128, 2048], F32, kind=k).ap()
    w["suT"] = nc.dram_tensor(f"{pfx}_suT", [11, 128, 2048], F32, kind=k).ap()
    w["sdT"] = nc.dram_tensor(f"{pfx}_sdT", [11, 128, 2048], F32, kind=k).ap()
    w["wgT"] = nc.dram_tensor(f"{pfx}_wgT", [8, 11, 128, 2048], F32, kind=k).ap()
    w["wuT"] = nc.dram_tensor(f"{pfx}_wuT", [8, 11, 128, 2048], F32, kind=k).ap()
    w["wdT"] = nc.dram_tensor(f"{pfx}_wdT", [8, 11, 128, 2048], F32, kind=k).ap()
    w["rwT"] = nc.dram_tensor(f"{pfx}_rwT", [128, 16 * 8], F32, kind=k).ap()
    return w


def moe_weight_arrays(inputs, pfx="m0"):
    out = {}
    out[f"{pfx}_sgT"] = prep_T_lhs(inputs["L0_ffn_shared_gate"])
    out[f"{pfx}_suT"] = prep_T_lhs(inputs["L0_ffn_shared_up"])
    out[f"{pfx}_sdT"] = prep_T(inputs["L0_ffn_shared_down"])
    out[f"{pfx}_wgT"] = np.stack([prep_T_lhs(np.asarray(inputs["L0_ffn_w_gate"])[e]) for e in range(8)])
    out[f"{pfx}_wuT"] = np.stack([prep_T_lhs(np.asarray(inputs["L0_ffn_w_up"])[e]) for e in range(8)])
    out[f"{pfx}_wdT"] = np.stack([prep_T(np.asarray(inputs["L0_ffn_w_down"])[e]) for e in range(8)])
    rwT = prep_T(inputs["L0_ffn_router_w"])
    out[f"{pfx}_rwT"] = np.ascontiguousarray(rwT.transpose(1, 0, 2).reshape(128, 16 * 8))
    return out



def build_proj_residual(p, mixT_ap, w_ap, xin_ap, xout_ap, sb):
    xt = sb["xt"]; wgb = sb["wgb"]; pd = sb["pd"]
    wob = sb["u1"][:, 0:KD * D]
    p.guard('gpsimd', "u1")
    for g in range(KD):
        vw = p.dma('gpsimd', wob[:, g * D:(g + 1) * D],
                   w_ap[g * 128:(g + 1) * 128, :], sem="s_wo")
    vwo = p.cnt["s_wo"]
    p.wait('tensor', "s_wo", vwo)
    for t in range(NTOK // 128):
        mb = wgb[t % 2]
        p.guard('sync', id(mb))
        vm = p.dma('sync', mb[:, :].rearrange("q (g c) -> q g c", g=KD),
                   mixT_ap[:, t * 128:(t + 1) * 128].rearrange(
                       "(g q) c -> q g c", q=128), sem=f"s_mx{t % 2}")
        buf = xt[t % 2]
        p.guard('sync', id(buf))
        p.guard('sync', (id(buf), 'st'))
        vx = p.dma('sync', buf[:, :], xin_ap[t * 128:(t + 1) * 128, :],
                   sem=f"s_x{t % 2}")
        p.wait('tensor', f"s_mx{t % 2}", vm)
        lastev = None
        for n in range(4):
            ps = pd[n % 2]
            p.guard('tensor', id(ps))
            vdn = acc_group(p, ps[:, :],
                            [(mb[:, g * 128:(g + 1) * 128],
                              wob[:, g * D + n * 512: g * D + (n + 1) * 512])
                             for g in range(KD)],
                            sem="s_dnmm")
            p.wait('vector', "s_dnmm", vdn)
            p.wait('vector', f"s_x{t % 2}", vx)
            lastev = p.emit('vector',
                            lambda v, ps=ps, b=buf, n=n:
                            v.tensor_add(b[:, n * 512:(n + 1) * 512],
                                         b[:, n * 512:(n + 1) * 512], ps[:, :]),
                            sem="s_dn")
            hv = p.hop('scalar', "s_dn", lastev)
            p.guard_set(id(ps), *hv)
        p.guard_set(id(mb), "s_dnmm", p.cnt["s_dnmm"])
        p.wait('sync', "s_dn", lastev)
        vst = p.dma('sync', xout_ap[t * 128:(t + 1) * 128, :], buf[:, :],
                    sem=f"s_st{t % 2}")
        p.guard_set(id(buf), "s_dn", lastev)
        p.guard_set((id(buf), 'st'), f"s_st{t % 2}", vst)
    p.guard_set("u1", "s_dnmm", p.cnt["s_dnmm"])
    p.guard_set("u1", "s_wo", p.cnt["s_wo"])



KI2 = 5632 // 128


def build_l1ffn_sublayer(p, xin_ap, xout_ap, lnoutT_ap, w, sb, final_store=None):
    xt = sb["xt"]; scr = sb["scr"]; xlnbf = sb["xlnbf"]; xlnT = sb["xlnT"]
    wgb = sb["wgb"]; wub = sb["wub"]; pt = sb["pt"]; prt = sb["prt"]
    pg = sb["pg"]; pu = sb["pu"]; pd = sb["pd"]
    sgev = sb["sgev"]; lnstage = sb["lnstage"]; ident = sb["ident"]
    mean, rstd = sb["mean"], sb["rstd"]
    mean2, rstd2 = sb["mean2"], sb["rstd2"]
    h44 = sb["u1"][:, 0:KI2 * NT]
    if "l1_sdb" not in sb:
        sb["l1_sdb"] = p.sbuf("l1_sdb", [128, D], F32)
        sb["l1_sg"] = p.sbuf("l1_sg", [128, 64], F32)
        sb["l1_su"] = p.sbuf("l1_su", [128, 64], F32)
        sb["l1_rows"] = p.sbuf("l1_rows", [128, 1024], BF16)
        sb["onesb"] = p.sbuf("onesb", [128, 512], BF16)
        sb["onesf"] = p.sbuf("onesf", [128, 128], F32)
        p.emit('gpsimd', lambda g, o=sb["onesb"]: g.memset(o[:, :], 1.0), sem="s_ones")
        von = p.emit('gpsimd', lambda g, o=sb["onesf"]: g.memset(o[:, :], 1.0), sem="s_ones")
        p.wait('tensor', "s_ones", von)
        p.wait('scalar', "s_ones", von)
        sb["l1_sdrow"] = p.sbuf("l1_sdrow", [1, D], F32)
        p.dma('sync', sb["l1_sg"][:, 0:KI2], w["sg"], sem="s_sc")
        p.dma('sync', sb["l1_su"][:, 0:KI2], w["su"], sem="s_sc")
        p.dma('sync', sb["l1_sdrow"][0:1, :], w["sd"], sem="s_sc")
        vsc = p.cnt["s_sc"]
        p.wait('tensor', "s_sc", vsc)
        p.wait('scalar', "s_sc", vsc)
        p.wait('vector', "s_sc", vsc)
        for n in range(4):
            p.guard('tensor', id(pd[0]))
            vb = p.emit('tensor', lambda t, n=n, o=sb["onesf"], r=sb["l1_sdrow"], ps=pd[0]:
                        t.matmul(ps[:, :], o[0:1, 0:128], r[0:1, n * 512:(n + 1) * 512],
                                 start=True, stop=True), sem="s_dnmm")
            p.wait('scalar', "s_dnmm", vb)
            vse = p.emit('scalar', lambda s_, n=n, ps=pd[0], sdb=sb["l1_sdb"]:
                         s_.copy(out=sdb[:, n * 512:(n + 1) * 512], in_=ps[:, :]),
                         sem="s_dnS")
            hv = p.hop('vector', "s_dnS", vse)
            p.guard_set(id(pd[0]), *hv)
        p.wait_cur('vector', "s_dnS")
        vhalf = p.emit('vector', lambda v, sdb=sb["l1_sdb"]:
                       v.tensor_scalar_mul(sdb[:, :], sdb[:, :], 0.5), sem="s_dn")
        p.guard_set("l1_sdb", "s_dn", vhalf)
    sdb = sb["l1_sdb"]; sgall = sb["l1_sg"]; suall = sb["l1_su"]
    rows = sb["l1_rows"]; onesb = sb["onesb"]
    crow_x = rows[0:1, 0:NT]
    crow_h = rows[0:1, NT:2 * NT]

    for s in range(NSUB):
        tok0 = s * NT
        for t in range(TT):
            buf = xt[t % 2]
            p.guard('sync', id(buf))
            p.guard('sync', (id(buf), 'st'))
            sx = f"s_x{t % 2}"
            vx = p.dma('sync', buf[:, :],
                       xin_ap[tok0 + t * 128: tok0 + (t + 1) * 128, :], sem=sx)
            p.wait('vector', sx, vx)
            p.wait('scalar', sx, vx)
            p.guard('scalar', id(xlnT))
            p.guard('vector', id(xlnbf))
            vln = emit_ln_tile(p, buf[:, :], xlnbf[:, :], scr[:, :],
                               mean[:, 4 * t:4 * t + 1], rstd[:, 4 * t:4 * t + 1], "s_ln")
            p.guard_set(id(buf), "s_ln", vln)
            p.wait_cur('tensor', "s_ln")
            emit_transpose_tile(
                p, xlnbf,
                (lambda g, t=t: xlnT[:, g * NT + t * 128: g * NT + (t + 1) * 128]),
                pt, ident[:, :], "s_tp", "s_te")
            p.guard_set(id(xlnbf), "s_tp", p.cnt["s_tp"])
        p.wait_cur('tensor', "s_te")
        p.guard('tensor', id(prt))
        vcs = acc_group(p, prt[0:1, 0:NT],
                        [(onesb[:, 0:1], xlnT[:, g * NT:(g + 1) * NT])
                         for g in range(KD)], sem="s_rt")
        p.wait('scalar', "s_rt", vcs)
        p.guard('scalar', "l1_rows")
        vcr = p.emit('scalar', lambda s_, ps=prt, cr=crow_x:
                     s_.activation(out=cr, in_=ps[0:1, 0:NT], func=AF.Copy,
                                   scale=-128.0), sem="s_rts", drain_inc=True)
        hv = p.hop('vector', "s_rts", vcr)
        p.guard_set(id(prt), *hv)
        p.wait('tensor', "s_rts", vcr)
        p.wait('sync', "s_cast", w["castv"])
        for j in range(KI2):
            par = j % 2
            gb, ub = wgb[par], wub[par]
            pgj, puj = pg[par], pu[par]
            p.guard('sync', id(gb))
            vg = p.dma('sync', gb[:, :], w["gT"][j], sem=f"s_qg{par}")
            p.guard('sync', id(ub))
            vu = p.dma('sync', ub[:, :], w["uT"][j], sem=f"s_qu{par}")
            p.wait('tensor', f"s_qg{par}", vg)
            p.wait('tensor', f"s_qu{par}", vu)
            p.wait('tensor', "s_cast", w["castv"])
            p.guard('tensor', id(pgj))
            vgm = acc_group(p, pgj[:, :],
                            [(gb[:, g * 128:(g + 1) * 128],
                              xlnT[:, g * NT:(g + 1) * NT]) for g in range(KD)]
                            + [(onesb[0:1, 0:128], crow_x)],
                            sem="s_gu")
            p.guard_set(id(gb), "s_gu", vgm)
            p.guard('tensor', id(puj))
            vum = acc_group(p, puj[:, :],
                            [(ub[:, g * 128:(g + 1) * 128],
                              xlnT[:, g * NT:(g + 1) * NT]) for g in range(KD)]
                            + [(onesb[0:1, 0:128], crow_x)],
                            sem="s_gu")
            p.guard_set(id(ub), "s_gu", vum)
            p.wait('scalar', "s_gu", vum)
            p.guard('scalar', id(sgev))
            vsg = p.emit('scalar', lambda sc, ps=pgj, sg=sgev, j=j, sa=sgall:
                         sc.activation(out=sg[:, 0:NT], in_=ps[:, :], func=AF.Silu,
                                       scale=sa[:, j:j + 1]), sem="s_hs")
            p.wait('vector', "s_hs", vsg)
            p.guard('vector', "u1")
            p.guard('vector', ("h44", j))
            vh = p.emit('vector', lambda v, sg=sgev, ps=puj, hb=h44, j=j, su=suall:
                        v.scalar_tensor_tensor(out=hb[:, j * NT:(j + 1) * NT],
                                               in0=ps[:, :], scalar=su[:, j:j + 1],
                                               in1=sg[:, 0:NT], op0=ALU.mult,
                                               op1=ALU.mult), sem="s_h")
            hv = p.hop('scalar', "s_h", vh)
            p.guard_set(id(pgj), *hv)
            p.guard_set(id(puj), *hv)
            p.guard_set(id(sgev), *hv)
        p.wait_cur('tensor', "s_h")
        p.guard('tensor', id(prt))
        vcs2 = acc_group(p, prt[0:1, 0:NT],
                         [(onesb[:, 0:1], h44[:, j * NT:(j + 1) * NT])
                          for j in range(KI2)], sem="s_rt")
        p.wait('scalar', "s_rt", vcs2)
        vcr2 = p.emit('scalar', lambda s_, ps=prt, cr=crow_h:
                      s_.activation(out=cr, in_=ps[0:1, 0:NT], func=AF.Copy,
                                    scale=-128.0), sem="s_rts", drain_inc=True)
        p.guard_set("l1_rows", "s_rts", vcr2)
        hv = p.hop('vector', "s_rts", vcr2)
        p.guard_set(id(prt), *hv)
        p.wait('tensor', "s_rts", vcr2)
        ptf = pt[:, :].bitcast(F32)
        banks = [pg[0][:, :], pg[1][:, :], pu[0][:, :], pu[1][:, :],
                 pd[0][:, :], pd[1][:, :], prt[:, :], ptf]
        bankkey = [id(pg[0]), id(pg[1]), id(pu[0]), id(pu[1]),
                   id(pd[0]), id(pd[1]), id(prt), id(pt)]
        for blk in range(2):
            bt0 = blk * 2
            bufs = []
            for bi in range(2):
                buf = xt[bi]
                p.guard('sync', id(buf))
                p.guard('sync', (id(buf), 'st'))
                vx = p.dma('sync', buf[:, :],
                           xin_ap[tok0 + (bt0 + bi) * 128: tok0 + (bt0 + bi + 1) * 128, :],
                           sem=f"s_x{bi}")
                bufs.append((buf, vx, f"s_x{bi}"))
            for j in range(KI2):
                par = j % 2
                db = wgb[par]
                p.guard('sync', id(db))
                vd = p.dma('sync', db[:, :], w["dT"][j], sem=f"s_wdn{par}")
                p.wait('tensor', f"s_wdn{par}", vd)
                for k8 in range(8):
                    bi, n = k8 // 4, k8 % 4
                    ps = banks[k8]
                    if j == 0:
                        p.guard('tensor', bankkey[k8])

                    def dm(t, ps=ps, hb=h44, j=j, tt=bt0 + bi, n=n, db=db):
                        return t.matmul(ps,
                                        hb[:, j * NT + tt * 128: j * NT + (tt + 1) * 128],
                                        db[:, n * 512:(n + 1) * 512],
                                        start=(j == 0), stop=False)
                    if k8 == 7:
                        vdw = p.emit('tensor', dm, sem="s_dnw")
                        p.guard_set(id(db), "s_dnw", vdw)
                    else:
                        p.emit('tensor', dm)
            for k8 in range(8):
                bi, n = k8 // 4, k8 % 4
                ps = banks[k8]
                buf, vx, sxn = bufs[bi]
                t_tile = bt0 + bi
                vdn = p.emit('tensor', lambda t, ps=ps, cr=crow_h, tt=t_tile, ob=onesb:
                             t.matmul(ps, cr[0:1, tt * 128:(tt + 1) * 128],
                                      ob[0:1, 0:512], start=False, stop=True),
                             sem="s_dnmm")
                p.wait('vector', "s_dnmm", vdn)
                p.wait('vector', sxn, vx)
                p.guard('vector', "l1_sdb")

                def ev(v, ps=ps, b=buf, n=n, sdb=sdb, sc=scr):
                    v.tensor_tensor(out=sc[:, n * 512:(n + 1) * 512], in0=ps,
                                    in1=sdb[:, n * 512:(n + 1) * 512], op=ALU.mult)
                    v.drain()
                    return v.tensor_add(b[:, n * 512:(n + 1) * 512],
                                        b[:, n * 512:(n + 1) * 512],
                                        sc[:, n * 512:(n + 1) * 512])
                vev = p.emit('vector', ev, sem="s_dn")
                hv = p.hop('scalar', "s_dn", vev)
                p.guard_set(bankkey[k8], *hv)
            for bi in range(2):
                buf, vx, sxn = bufs[bi]
                t_tile = bt0 + bi
                p.wait_cur('sync', "s_dn")
                vst = p.dma('sync',
                            xout_ap[tok0 + t_tile * 128: tok0 + (t_tile + 1) * 128, :],
                            buf[:, :], sem=f"s_st{bi}")
                p.guard_set((id(buf), 'st'), f"s_st{bi}", vst)
                if lnoutT_ap is not None:
                    p.wait_cur('scalar', "s_dn")
                    p.guard('scalar', id(lnstage))
                    p.guard('vector', id(xlnbf))
                    vln2 = emit_ln_tile(p, buf[:, :], xlnbf[:, :], scr[:, :],
                                        mean2[:, 4 * t_tile:4 * t_tile + 1],
                                        rstd2[:, 4 * t_tile:4 * t_tile + 1], "s_ln")
                    p.guard_set(id(buf), "s_ln", vln2)
                    p.wait_cur('tensor', "s_ln")
                    emit_transpose_tile(
                        p, xlnbf,
                        (lambda g, t=t_tile: lnstage[:, g * NT + t * 128: g * NT + (t + 1) * 128]),
                        pt, ident[:, :], "s_tp", "s_te")
                    p.guard_set(id(xlnbf), "s_tp", p.cnt["s_tp"])
                else:
                    p.guard_set(id(buf), "s_dn", p.cnt["s_dn"])
        for j in range(KI2):
            p.guard_set(("h44", j), "s_dnmm", p.cnt["s_dnmm"])
        if lnoutT_ap is not None:
            p.wait_cur('sync', "s_te")
            for g in range(KD):
                p.dma('sync', lnoutT_ap[g * 128:(g + 1) * 128, tok0:tok0 + NT],
                      lnstage[:, g * NT:(g + 1) * NT], sem="s_lnst")
            p.guard_set(id(lnstage), "s_lnst", p.cnt["s_lnst"])
        p.guard_set(id(xlnT), "s_gu", p.cnt["s_gu"])
        p.guard_set(id(prt), "s_rt", p.cnt["s_rt"])
    p.guard_set("u1", "s_dnmm", p.cnt["s_dnmm"])
    p.guard_set("u1", "s_h", p.cnt["s_h"])


def l1ffn_weight_aps(nc, pfx="q"):
    k = "ExternalInput"
    w = {}
    w["gT_i"] = nc.dram_tensor(f"{pfx}_gTi", [KI2, 128, 2048], I32, kind=k).ap()
    w["uT_i"] = nc.dram_tensor(f"{pfx}_uTi", [KI2, 128, 2048], I32, kind=k).ap()
    w["dT_i"] = nc.dram_tensor(f"{pfx}_dTi", [KI2, 128, 2048], I32, kind=k).ap()
    w["gT"] = nc.dram_tensor(f"{pfx}_gTb", [KI2, 128, 2048], BF16).ap()
    w["uT"] = nc.dram_tensor(f"{pfx}_uTb", [KI2, 128, 2048], BF16).ap()
    w["dT"] = nc.dram_tensor(f"{pfx}_dTb", [KI2, 128, 2048], BF16).ap()
    w["sg"] = nc.dram_tensor(f"{pfx}_sg", [128, KI2], F32, kind=k).ap()
    w["su"] = nc.dram_tensor(f"{pfx}_su", [128, KI2], F32, kind=k).ap()
    w["sd"] = nc.dram_tensor(f"{pfx}_sd", [1, D], F32, kind=k).ap()
    return w


def l1ffn_prepass(p, w):
    p.dma('gpsimd', w["gT"], w["gT_i"], sem="s_cast")
    p.dma('gpsimd', w["uT"], w["uT_i"], sem="s_cast")
    p.dma('gpsimd', w["dT"], w["dT_i"], sem="s_cast")
    w["castv"] = p.cnt["s_cast"]


def l1ffn_weight_arrays(inputs, pfx="q"):
    out = {}
    out[f"{pfx}_gTi"] = prep_T_lhs(np.asarray(inputs["L1_ffn_wq_gate"]))
    out[f"{pfx}_uTi"] = prep_T_lhs(np.asarray(inputs["L1_ffn_wq_up"]))
    out[f"{pfx}_dTi"] = prep_T(np.asarray(inputs["L1_ffn_wq_down"]))
    out[f"{pfx}_sg"] = np.ascontiguousarray(
        np.asarray(inputs["L1_ffn_s_gate"], np.float32).reshape(KI2, 128).T)
    out[f"{pfx}_su"] = np.ascontiguousarray(
        np.asarray(inputs["L1_ffn_s_up"], np.float32).reshape(KI2, 128).T)
    out[f"{pfx}_sd"] = np.asarray(inputs["L1_ffn_s_down"], np.float32).reshape(1, D)
    return out


def build_launch3(parts="pml"):
    nc = bass.Bass("TRN2", target_bir_lowering=False)
    x1 = nc.dram_tensor("x1", [NTOK, D], F32, kind="ExternalInput").ap()
    mixT = nc.dram_tensor("mixT", [D, NTOK], BF16, kind="ExternalInput").ap()
    Wo = nc.dram_tensor("Wo", [D, D], F32, kind="ExternalInput").ap()
    x2 = nc.dram_tensor("x2", [NTOK, D], F32, kind="ExternalOutput").ap()
    x3 = nc.dram_tensor("x3", [NTOK, D], F32, kind="ExternalOutput").ap()
    x4 = nc.dram_tensor("x4", [NTOK, D], F32, kind="ExternalOutput").ap()
    ln5T = nc.dram_tensor("ln5T", [D, NTOK], BF16, kind="ExternalOutput").ap()
    wm = moe_weight_aps(nc)
    wq = l1ffn_weight_aps(nc)
    with ExitStack() as stack:
        p = Prog(nc, stack)
        p.const(1e-5)
        sb = {}
        mk_shared_bufs(p, sb)
        if "l" in parts:
            l1ffn_prepass(p, wq)
        xcur = x1
        if "p" in parts:
            build_proj_residual(p, mixT, Wo, xcur, x2, sb)
            xcur = x2
        if "m" in parts:
            build_moe_sublayer(p, "m2", xcur, x3, None, wm, sb)
            xcur = x3
        if "l" in parts:
            build_l1ffn_sublayer(p, xcur, x4, ln5T, wq, sb)
        p.wait_cur('scalar', "s_st0")
        p.wait_cur('scalar', "s_st1")
        p.wait_cur('scalar', "s_lnst")
        p.replay()
    return nc


def build_launch5():
    nc = bass.Bass("TRN2", target_bir_lowering=False)
    x4 = nc.dram_tensor("x4", [NTOK, D], F32, kind="ExternalInput").ap()
    a1T = nc.dram_tensor("a1T", [D, NTOK], BF16, kind="ExternalInput").ap()
    Wp = nc.dram_tensor("Wp", [D, D], F32, kind="ExternalInput").ap()
    x5 = nc.dram_tensor("x5", [NTOK, D], F32, kind="ExternalOutput").ap()
    x6 = nc.dram_tensor("x6", [NTOK, D], F32, kind="ExternalOutput").ap()
    wq = l1ffn_weight_aps(nc)
    with ExitStack() as stack:
        p = Prog(nc, stack)
        p.const(1e-5)
        sb = {}
        mk_shared_bufs(p, sb)
        l1ffn_prepass(p, wq)
        build_proj_residual(p, a1T, Wp, x4, x5, sb)
        build_l1ffn_sublayer(p, x5, x6, None, wq, sb)
        p.wait_cur('scalar', "s_st0")
        p.wait_cur('scalar', "s_st1")
        p.replay()
    return nc



LB = 4096


def emit_rope_tables(p, sb):
    if "rp_cos" in sb:
        return
    cos = p.sbuf("rp_cos", [128, 32 * 64], F32)
    sin = p.sbuf("rp_sin", [128, 32 * 64], F32)
    freq = p.sbuf("rp_freq", [1, 64], F32)
    posr = p.sbuf("rp_pos", [1, 128], F32)
    scr2 = p.sbuf("rp_scr", [128, 2 * 64], F32)
    pf = sb["pd"][0]
    sb["rp_cos"], sb["rp_sin"] = cos, sin
    def g1(g, fr=freq):
        g.iota(fr[0:1, :], pattern=[[1, 64]], base=0, channel_multiplier=0,
               allow_small_or_imprecise_dtypes=True)
        return g.drain()
    v1 = p.emit('gpsimd', g1, sem="s_rp")
    p.wait('scalar', "s_rp", v1)
    c = float(-np.log(10000.0) / 64.0)
    v2 = p.emit('scalar', lambda s_, fr=freq: s_.activation(
        out=fr[0:1, :], in_=fr[0:1, :], func=AF.Exp, scale=c), sem="s_rps")
    TWO_PI = float(2 * np.pi)
    M23 = float(2 ** 23)
    for t in range(32):
        def g2(g, pr=posr, t=t):
            g.iota(pr[0:1, :], pattern=[[1, 128]], base=t * 128,
                   channel_multiplier=0,
                   allow_small_or_imprecise_dtypes=True)
            return g.drain()
        p.guard('gpsimd', "rp_pos")
        vp = p.emit('gpsimd', g2, sem="s_rp")
        p.wait('tensor', "s_rp", vp)
        p.wait('tensor', "s_rps", v2)
        p.guard('tensor', id(pf))
        vmm = p.emit('tensor', lambda t_, pr=posr, fr=freq, ps=pf:
                     t_.matmul(ps[:, 0:64], pr[0:1, :], fr[0:1, :],
                               start=True, stop=True), sem="s_rpt")
        p.guard_set("rp_pos", "s_rpt", vmm)
        p.wait('vector', "s_rpt", vmm)
        def vr(v, ps=pf, sc=scr2, t=t, cos=cos, sin=sin):
            v.tensor_scalar(out=sc[:, 0:64], in0=ps[:, 0:64],
                            scalar1=float(1.0 / TWO_PI), scalar2=M23,
                            op0=ALU.mult, op1=ALU.add)
            v.drain()
            v.tensor_scalar_add(sc[:, 0:64], sc[:, 0:64], -M23)
            v.drain()
            v.scalar_tensor_tensor(out=sc[:, 0:64], in0=sc[:, 0:64],
                                   scalar=-TWO_PI, in1=ps[:, 0:64],
                                   op0=ALU.mult, op1=ALU.add)
            v.drain()
            v.tensor_scalar_add(sc[:, 64:128], ps[:, 0:64], float(np.pi / 2))
            v.drain()
            v.tensor_scalar(out=cos[:, t * 64:(t + 1) * 64], in0=sc[:, 64:128],
                            scalar1=float(1.0 / TWO_PI), scalar2=M23,
                            op0=ALU.mult, op1=ALU.add)
            v.drain()
            v.tensor_scalar_add(cos[:, t * 64:(t + 1) * 64],
                                cos[:, t * 64:(t + 1) * 64], -M23)
            v.drain()
            v.scalar_tensor_tensor(
                out=sc[:, 64:128], in0=cos[:, t * 64:(t + 1) * 64],
                scalar=-TWO_PI, in1=sc[:, 64:128], op0=ALU.mult, op1=ALU.add)
            v.drain()
            PI = float(np.pi)
            v.tensor_scalar_min(sc[:, 0:128], sc[:, 0:128], PI)
            v.drain()
            return v.tensor_scalar_max(sc[:, 0:128], sc[:, 0:128], -PI)
        p.guard('vector', "rp_scr")
        vv = p.emit('vector', vr, sem="s_rpv", drain_inc=True)
        hv = p.hop('vector', "s_rpv", vv)
        p.guard_set(id(pf), *hv)
        p.wait('scalar', "s_rpv", vv)
        PI = float(np.pi)

        def sr(s_, sc=scr2, t=t, cos=cos, sin=sin, PI=PI):
            s_.activation(out=sin[:, t * 64:(t + 1) * 64], in_=sc[:, 0:64],
                          func=AF.Sin)
            return s_.activation(out=cos[:, t * 64:(t + 1) * 64],
                                 in_=sc[:, 64:128], func=AF.Sin)
        vsr = p.emit('scalar', sr, sem="s_rps")
        p.guard_set("rp_scr", "s_rps", vsr)
    p.guard_set("rope_tables", "s_rps", p.cnt["s_rps"])


def build_l1attn(p, lnT_ap, wq_ap, ropeRT_ap, outT_ap, sb):
    mk_shared_bufs(p, sb, big=False)
    ident = sb["ident"]
    pg = sb["pg"]; pu = sb["pu"]; pd = sb["pd"]; pt = sb["pt"]; prt = sb["prt"]
    emit_rope_tables(p, sb)
    cos, sin = sb["rp_cos"], sb["rp_sin"]

    if "fa_qkv" not in sb:
        sb["fa_qkv"] = p.sbuf("fa_qkv", [128, 32 * 1536], BF16)
        sb["fa_w"] = p.sbuf("fa_w", [128, KD * 1536], BF16)
        sb["fa_qT"] = p.sbuf("fa_qT", [128, LB], BF16)
        sb["fa_kT"] = p.sbuf("fa_kT", [128, LB], BF16)
        sb["fa_R"] = p.sbuf("fa_R", [128, 128], BF16)
        sb["fa_P"] = p.sbuf("fa_P", [128, LB], BF16)
        sb["fa_PT"] = p.sbuf("fa_PT", [128, 512], BF16)
        sb["fa_st"] = p.sbuf("fa_st", [128, 32], F32)
        sb["fa_o"] = p.sbuf("fa_o", [128, 128], BF16)
        sb["fa_out"] = p.sbuf("fa_out", [128, LB], BF16)
        sb["fa_rsc"] = p.sbuf("fa_rsc", [128, 2 * 512], F32)
    qkv = sb["fa_qkv"]; wbf = sb["fa_w"]; qT = sb["fa_qT"]; kT = sb["fa_kT"]
    Rm = sb["fa_R"]; Pb = sb["fa_P"]; PT = sb["fa_PT"]; st = sb["fa_st"]
    ob = sb["fa_o"]; outb = sb["fa_out"]; rsc = sb["fa_rsc"]

    p.guard('gpsimd', "fa_w")
    for g in range(KD):
        vw = p.dma('gpsimd', wbf[:, g * 1536:(g + 1) * 1536],
                   wq_ap[g * 128:(g + 1) * 128, :], sem="s_wo")
    vR = p.dma('gpsimd', Rm[:, :], ropeRT_ap, sem="s_wo")
    vwall = p.cnt["s_wo"]
    p.wait('tensor', "s_wo", vwall)
    SCL = float(1.0 / np.sqrt(128.0))

    for b in range(2):
        tokb = b * LB
        for t in range(32):
            p.guard('sync', ("fa_ln", t % 2))
            vl = p.dma('sync', sb["wgb"][t % 2][:, :].rearrange("q (g c) -> q g c", g=KD),
                       lnT_ap[:, tokb + t * 128: tokb + (t + 1) * 128].rearrange(
                           "(g q) c -> q g c", q=128), sem=f"s_mx{t % 2}")
            lnb = sb["wgb"][t % 2]
            p.wait('tensor', f"s_mx{t % 2}", vl)
            for n in range(3):
                ps = pd[n % 2]
                p.guard('tensor', id(ps))
                vmm = acc_group(p, ps[:, :],
                                [(lnb[:, g * 128:(g + 1) * 128],
                                  wbf[:, g * 1536 + n * 512: g * 1536 + (n + 1) * 512])
                                 for g in range(KD)], sem="s_dnmm")
                p.wait('vector', "s_dnmm", vmm)
                p.guard('vector', "rope_tables")

                def ev(v, ps=ps, t=t, n=n, qkv=qkv, cos=cos, sin=sin, rsc=rsc):
                    dst = qkv[:, t * 1536 + n * 512: t * 1536 + (n + 1) * 512]
                    if n == 2:
                        return v.tensor_copy(out=dst, in_=ps[:, :])
                    c64 = cos[:, t * 64:(t + 1) * 64]
                    s64 = sin[:, t * 64:(t + 1) * 64]
                    for hh in range(4):
                        pe = ps[:, hh * 128: (hh + 1) * 128]
                        de = dst[:, hh * 128:(hh + 1) * 128]
                        re = rsc[:, 0:512]
                        v.drain()
                        v.tensor_tensor(out=re[:, 0:128:2], in0=pe[:, 0:128:2],
                                        in1=c64, op=ALU.mult)
                        v.tensor_tensor(out=re[:, 1:128:2], in0=pe[:, 1:128:2],
                                        in1=c64, op=ALU.mult)
                        v.tensor_tensor(out=rsc[:, 512:1024][:, 0:128:2],
                                        in0=pe[:, 1:128:2], in1=s64, op=ALU.mult)
                        v.tensor_tensor(out=rsc[:, 512:1024][:, 1:128:2],
                                        in0=pe[:, 0:128:2], in1=s64, op=ALU.mult)
                        v.drain()
                        v.tensor_sub(de[:, 0:128:2], re[:, 0:128:2],
                                     rsc[:, 512:1024][:, 0:128:2])
                        v.tensor_add(de[:, 1:128:2], re[:, 1:128:2],
                                     rsc[:, 512:1024][:, 1:128:2])
                    return v.drain()
                vev = p.emit('vector', ev, sem="s_dn", drain_inc=True)
                hv = p.hop('scalar', "s_dn", vev)
                p.guard_set(id(ps), *hv)
            p.guard_set(("fa_ln", t % 2), "s_dnmm", p.cnt["s_dnmm"])
        p.guard_set("rope_done", "s_dn", p.cnt["s_dn"])
        for hh in range(4):
            qcol = hh * 128
            kcol = 512 + hh * 128
            vcol = 1024 + hh * 128
            p.wait_cur('tensor', "s_dn")
            for t in range(32):
                p.guard('tensor', id(pt))
                vtp = p.emit('tensor', lambda t_, t=t, qkv=qkv, qcol=qcol, pt=pt, idn=ident:
                             t_.transpose(pt[:, 0:128],
                                          qkv[:, t * 1536 + qcol: t * 1536 + qcol + 128],
                                          idn[:, :]), sem="s_tp")
                p.emit('tensor', lambda t_, t=t, qkv=qkv, kcol=kcol, pt=pt, idn=ident:
                       t_.transpose(pt[:, 128:256],
                                    qkv[:, t * 1536 + kcol: t * 1536 + kcol + 128],
                                    idn[:, :]), sem="s_tp")
                vtp = p.cnt["s_tp"]
                p.wait('scalar', "s_tp", vtp)
                def evq(s_, t=t, pt=pt, qT=qT, kT=kT):
                    s_.activation(out=qT[:, t * 128:(t + 1) * 128],
                                  in_=pt[:, 0:128], func=AF.Copy, scale=SCL)
                    return s_.copy(out=kT[:, t * 128:(t + 1) * 128],
                                   in_=pt[:, 128:256])
                vev = p.emit('scalar', evq, sem="s_te")
                hv = p.hop('vector', "s_te", vev)
                p.guard_set(id(pt), *hv)
            p.wait_cur('tensor', "s_te")
            ptf = pt[:, :].bitcast(F32)
            Sb = [pg[0][:, :], pg[1][:, :], pu[0][:, :], pu[1][:, :],
                  pd[0][:, :], pd[1][:, :], prt[:, :], ptf]
            Skey = [id(pg[0]), id(pg[1]), id(pu[0]), id(pu[1]),
                    id(pd[0]), id(pd[1]), id(prt), id(pt)]
            for qt in range(32):
                for kk in range(8):
                    p.guard('tensor', Skey[kk])
                    p.emit('tensor', lambda t_, qt=qt, kk=kk, qT=qT, kT=kT, ps=Sb[kk]:
                           t_.matmul(ps, qT[:, qt * 128:(qt + 1) * 128],
                                     kT[:, kk * 512:(kk + 1) * 512],
                                     start=True, stop=True), sem="s_fs")
                vsl = p.cnt["s_fs"]
                p.wait('vector', "s_fs", vsl)
                def vmax(v, Sb=Sb, st=st):
                    for kk in range(8):
                        v.reduce_max(out=st[:, kk:kk + 1], in_=Sb[kk], axis=AX.X)
                    v.drain()
                    v.reduce_max(out=st[:, 8:9], in_=st[:, 0:8], axis=AX.X,
                                 negate=True)
                    return v.drain()
                vmx = p.emit('vector', vmax, sem="s_fsv", drain_inc=True)
                p.wait('scalar', "s_fsv", vmx)
                def sexp(s_, Sb=Sb, st=st, Pb=Pb):
                    last = None
                    for kk in range(8):
                        last = s_.activation(out=Pb[:, kk * 512:(kk + 1) * 512],
                                             in_=Sb[kk], func=AF.Exp,
                                             bias=st[:, 8:9],
                                             accum_out=st[:, 16 + kk:17 + kk])
                    return last
                vex = p.emit('scalar', sexp, sem="s_fss", drain_inc=True)
                hv = p.hop('vector', "s_fss", vex)
                for kk in range(8):
                    p.guard_set(Skey[kk], *hv)
                def vsum(v, st=st):
                    v.tensor_reduce(out=st[:, 24:25], in_=st[:, 16:24], axis=AX.X,
                                    op=ALU.add)
                    v.drain()
                    return v.reciprocal(out=st[:, 25:26], in_=st[:, 24:25])
                vls = p.emit('vector', vsum, sem="s_fsv", drain_inc=True)
                p.wait('tensor', "s_fss", vex)
                p.guard('tensor', ("fa_pv", 0))
                for kk in range(8):
                    p.guard('tensor', ("fa_ptb", 0))
                    for sub in range(4):
                        p.emit('tensor', lambda t_, kk=kk, sub=sub, Pb=Pb, pt=pt, idn=ident:
                               t_.transpose(pt[:, 512 + sub * 128: 512 + (sub + 1) * 128],
                                            Pb[:, kk * 512 + sub * 128: kk * 512 + (sub + 1) * 128],
                                            idn[:, :]), sem="s_tp")
                    vtl = p.cnt["s_tp"]
                    p.wait('vector', "s_tp", vtl)
                    vpt = p.emit('vector', lambda v, pt=pt, PT=PT:
                                 v.tensor_copy(out=PT[:, :], in_=pt[:, 512:1024]),
                                 sem="s_pt")
                    hv = p.hop('scalar', "s_pt", vpt)
                    p.guard_set(("fa_ptb", 0), *hv)
                    p.wait('tensor', "s_pt", vpt)
                    p.emit('tensor', lambda t_, kk=kk, qkv=qkv, vcol=vcol, PT=PT, ps=pg[0]:
                           _pv_mm(t_, kk, qkv, vcol, PT, ps), sem="s_pv")
                vpvl = p.cnt["s_pv"]
                p.wait('scalar', "s_pv", vpvl)
                p.wait('scalar', "s_fsv", vls)
                def evo(s_, ob=ob, st=st, ps=pg[0]):
                    return s_.activation(out=ob[:, :], in_=ps[:, 0:128],
                                         func=AF.Copy, scale=st[:, 25:26])
                veo = p.emit('scalar', evo, sem="s_te", drain_inc=True)
                hv = p.hop('vector', "s_te", veo)
                p.guard_set(("fa_pv", 0), *hv)
                p.guard_set(id(pg[0]), *hv)
                p.wait('tensor', "s_te", veo)
                p.guard('tensor', ("fa_ob", 0))
                p.guard('tensor', id(pt))
                vot = p.emit('tensor', lambda t_, ob=ob, pt=pt, idn=ident:
                             t_.transpose(pt[:, 0:128], ob[:, :], idn[:, :]),
                             sem="s_tp")
                p.wait('vector', "s_tp", vot)
                voc = p.emit('vector', lambda v, qt=qt, pt=pt, outb=outb:
                             v.tensor_copy(out=outb[:, qt * 128:(qt + 1) * 128],
                                           in_=pt[:, 0:128]), sem="s_oc")
                hv = p.hop('scalar', "s_oc", voc)
                p.guard_set(id(pt), *hv)
                p.guard_set(("fa_ob", 0), "s_oc", p.cnt["s_oc"])
            p.wait_cur('sync', "s_oc")
            p.dma('sync', outT_ap[hh * 128:(hh + 1) * 128, tokb:tokb + LB],
                  outb[:, :], sem="s_lnst")
            p.guard_set("fa_out", "s_lnst", p.cnt["s_lnst"])
            p.guard('vector', "fa_out")
        p.guard_set("fa_qkv", "s_pv", p.cnt["s_pv"])
        p.guard_set("fa_qkv", "s_tp", p.cnt["s_tp"])
        p.guard('sync', "fa_qkv")


def _pv_mm(t_, kk, qkv, vcol, PT, ps):
    last = None
    for sub in range(4):
        tok = kk * 4 + sub
        last = t_.matmul(ps[:, 0:128],
                         PT[:, sub * 128:(sub + 1) * 128],
                         qkv[:, tok * 1536 + vcol: tok * 1536 + vcol + 128],
                         start=(kk == 0 and sub == 0),
                         stop=(kk == 7 and sub == 3))
    return last


def build_launch4():
    nc = bass.Bass("TRN2", target_bir_lowering=False)
    lnT = nc.dram_tensor("lnT", [D, 2 * LB], BF16, kind="ExternalInput").ap()
    wq = nc.dram_tensor("wq_cols", [D, 1536], F32, kind="ExternalInput").ap()
    rR = nc.dram_tensor("ropeRT", [128, 128], F32, kind="ExternalInput").ap()
    outT = nc.dram_tensor("a1T", [512, 2 * LB], BF16, kind="ExternalOutput").ap()
    with ExitStack() as stack:
        p = Prog(nc, stack)
        p.const(1e-5)
        sb = {}
        build_l1attn(p, lnT, wq, rR, outT, sb)
        p.wait_cur('scalar', "s_lnst")
        p.replay()
    return nc



CH = 1024
NCH = 4


def build_l0attn(p, io, sb, units=2, dbg=None):
    nc = p.nc

    def dbg_dump(key, cc, iw, src_ap):
        if dbg is None:
            return
        p.wait_cur('sync', "s_u3v")
        vd = p.dma('sync', dbg[key][cc, iw], src_ap, sem="s_dbg")
        for e in ('vector', 'scalar', 'tensor'):
            p.wait(e, "s_dbg", vd)
    mk_shared_bufs(p, sb, big=False, psums=False)
    ident = sb["ident"]

    pw = p.psum("l0_pw", [128, 1536], F32)
    b3 = p.psum("l0_b3", [128, 512], F32)
    b4 = p.psum("l0_b4", [128, 512], F32)
    b5 = p.psum("l0_b5", [128, 512], F32)
    b6 = p.psum("l0_b6", [128, 512], F32)
    pt = p.psum("l0_pt", [128, 1024], BF16)
    ptf = pt[:, :].bitcast(F32)
    pw0, pw1, pw2 = pw[:, 0:512], pw[:, 512:1024], pw[:, 1024:1536]
    PS4 = [pw0, pw1, pw2, b3[:, :]]
    PS4K = [("pw", 0), ("pw", 1), ("pw", 2), id(b3)]
    PS4B = [b4[:, :], b5[:, :], b6[:, :], ptf]
    PS4BK = [id(b4), id(b5), id(b6), id(pt)]

    wqb = p.sbuf("l0_wqb", [128, KD * 1536], BF16)
    exwb = p.sbuf("l0_exwb", [128, KD * 8], BF16)
    exbias = p.sbuf("l0_exbias", [1, 8], F32)
    exbc = p.sbuf("l0_exbc", [128, 8], F32)
    extras = p.sbuf("l0_ex", [128, 5 * 32], F32)
    onormg = p.sbuf("l0_ong", [1, 512], F32)
    ongbc = p.sbuf("l0_ongbc", [128, 512], F32)
    qkso = p.sbuf("l0_qkso", [128, 16], F32)
    qksop = p.sbuf("l0_qksop", [128, 16], F32)
    onesf = p.sbuf("l0_onesf", [128, 128], F32)
    tsc = p.sbuf("l0_tsc", [128, 512], F32)
    tsc2 = p.sbuf("l0_tsc2", [128, 512], F32)
    tst = p.sbuf("l0_tst", [128, 16], F32)
    och = p.sbuf("l0_och", [128, 512], F32)
    wst = p.sbuf("l0_wst", [128, 8], F32)
    W0 = p.sbuf("l0_W0", [128, 2048], F32)
    W1 = p.sbuf("l0_W1", [128, 2048], F32)
    W2 = p.sbuf("l0_W2", [128, 2048], F32)
    WN = (W0, W1, W2)
    Wn = p.sbuf("l0_Wn", [128, 16], F32)
    WM1 = p.sbuf("l0_WM1", [128, 1152], BF16)
    WM3 = p.sbuf("l0_WM3", [128, 1152], BF16)
    W0T = p.sbuf("l0_W0T", [128, 2048], BF16)
    W1T = p.sbuf("l0_W1T", [128, 2048], BF16)
    W2T = p.sbuf("l0_W2T", [128, 2048], BF16)
    W1N = p.sbuf("l0_W1N", [128, 2048], BF16)
    WTs = (W0T, W1T, W2T)
    dwf = p.sbuf("l0_dwf", [128, 2048], F32)
    dwa = [p.sbuf(f"l0_dwa{i}", [128, 2048], F32) for i in range(3)]
    mmv = p.sbuf("l0_mmv", [128, 4], F32)

    AR = p.sbuf("l0_ar", [128, 29184], BF16)

    def arb(off_b, n):
        assert off_b % 2 == 0
        return AR[:, off_b // 2: off_b // 2 + n]

    def arf(off_b, n):
        return AR[:, :].bitcast(F32)[:, off_b // 4: off_b // 4 + n]

    sA = arb(0, 2048); sB = arb(4096, 2048); sC = arb(8192, 2048)
    sD = arb(12288, 2048)
    gba = arf(16384, 2048)
    tmp = arf(24576, 2048)
    sgb = arb(32768, 2048); hbmb = arb(36864, 2048); hidb = arb(40960, 2048)
    dhb = arb(45056, 2048); ghb = arb(49152, 2048); tmpb = arb(53248, 2048)
    Xb = arb(0, 2048); Yb = arb(4096, 2048); Ab = arb(8192, 2048)
    Bbf = arb(12288, 2048); bAf = arf(16384, 2048)
    Mbf = arf(24576, 2048)
    wqf = arb(0, 4096); wkf = arb(8192, 4096); wvt = arb(16384, 4096)
    wtok = arb(24576, 4096)
    wP = arb(32768, 1280); wPT = arb(35840, 1280)
    wsc = arf(40960, 1536); wob = arf(47104, 128)
    stg = arb(0, 1536); stg2 = arb(3072, 1536); stgT = arb(6144, 1536)
    stg3 = arb(9216, 1536)

    raw_d = nc.dram_tensor("l0r_raw", [SEQ, 1536], BF16).ap()
    qnT_d = nc.dram_tensor("l0r_qnT", [32, 4, 128, 128], BF16).ap()
    knT_d = nc.dram_tensor("l0r_knT", [32, 4, 128, 128], BF16).ap()
    vT_d = nc.dram_tensor("l0r_vT", [32, 4, 128, 128], BF16).ap()
    kl0_d = nc.dram_tensor("l0r_kl0", [SEQ, 512], BF16).ap()
    kl2_d = nc.dram_tensor("l0r_kl2", [SEQ, 512], BF16).ap()
    vt_d = nc.dram_tensor("l0r_vt", [SEQ, 512], BF16).ap()
    win_d = nc.dram_tensor("l0r_win", [SEQ, 512], F32).ap()
    m_d = nc.dram_tensor("l0r_m", [3, 128, 2048], F32).ap()

    def gset(g):
        g.memset(onesf[:, :], 1.0)
        g.drain()
        g.memset(WM1[:, :], 0.0)
        g.drain()
        g.affine_select(out=WM1[:, :], in_=WM1[:, :], compare_op=ALU.is_ge,
                        fill=-1e30, base=1024, channel_multiplier=1,
                        pattern=[[-1, 1152]])
        g.drain()
        g.tensor_copy(out=WM3[:, :], in_=WM1[:, :])
        g.drain()
        g.affine_select(out=WM3[:, :], in_=WM3[:, :], compare_op=ALU.is_ge,
                        fill=-1e30, base=0, channel_multiplier=-1,
                        pattern=[[1, 1152]])
        return g.drain()
    von = p.emit('gpsimd', gset, sem="s_ones")
    p.wait('tensor', "s_ones", von)
    p.wait('vector', "s_ones", von)
    p.guard('gpsimd', "l0_wqb")
    for g in range(KD):
        p.dma('gpsimd', wqb[:, g * 1536:(g + 1) * 1536],
              io["wq_cols"][g * 128:(g + 1) * 128, :], sem="s_wo")
    for g in range(KD):
        p.dma('gpsimd', exwb[:, g * 8:(g + 1) * 8],
              io["ex_w"][g * 128:(g + 1) * 128, :], sem="s_wo")
    vwq = p.cnt["s_wo"]
    p.wait('tensor', "s_wo", vwq)
    p.dma('sync', exbias[0:1, :], io["ex_b"], sem="s_sc")
    p.dma('sync', onormg[0:1, :], io["onorm_g"], sem="s_sc")
    p.dma('sync', qkso[:, :].rearrange("q (a c) -> q a c", a=4),
          io["qk_so"].rearrange("(a q) c -> q a c", q=128), sem="s_sc")
    vsc = p.cnt["s_sc"]
    p.wait('tensor', "s_sc", vsc)
    p.wait('vector', "s_sc", vsc)
    p.guard('tensor', id(b3))
    vb1 = p.emit('tensor', lambda t: t.matmul(b3[:, 0:8], onesf[0:1, 0:128],
                                              exbias[0:1, :], start=True,
                                              stop=True), sem="s_bc")
    p.wait('vector', "s_bc", vb1)
    vb2 = p.emit('vector', lambda v: v.tensor_copy(out=exbc[:, :],
                                                   in_=b3[:, 0:8]), sem="s_bcv")
    hv = p.hop('scalar', "s_bcv", vb2)
    p.guard_set(id(b3), *hv)
    p.guard('tensor', id(b4))
    vb3 = p.emit('tensor', lambda t: t.matmul(b4[:, :], onesf[0:1, 0:128],
                                              onormg[0:1, :], start=True,
                                              stop=True), sem="s_bc")
    p.wait('vector', "s_bc", vb3)
    vb4 = p.emit('vector', lambda v: v.tensor_copy(out=ongbc[:, :],
                                                   in_=b4[:, :]), sem="s_bcv")
    hv = p.hop('scalar', "s_bcv", vb4)
    p.guard_set(id(b4), *hv)
    SC128 = float(1.0 / np.sqrt(128.0))

    def vqk(v):
        v.tensor_copy(out=qksop[:, :], in_=qkso[:, :])
        v.drain()
        for a in range(4):
            v.tensor_scalar_mul(qksop[:, a * 4:a * 4 + 1],
                                qksop[:, a * 4:a * 4 + 1], SC128)
            v.tensor_scalar_mul(qksop[:, a * 4 + 2:a * 4 + 3],
                                qksop[:, a * 4 + 2:a * 4 + 3], SC128)
        return v.drain()
    p.emit('vector', vqk, sem="s_bcv", drain_inc=True)

    def foldW(Wb, kg, mg):
        return Wb[:, kg * 512 + mg * 128: kg * 512 + mg * 128 + 128]

    def transpose_2048(src_bf, dst_bf):
        for half in range(2):
            p.guard('tensor', id(pt))

            def tf(t_, half=half):
                last = None
                for i in range(8):
                    blk = half * 8 + i
                    gr, gc = blk // 4, blk % 4
                    last = t_.transpose(
                        pt[:, i * 128:(i + 1) * 128],
                        src_bf[:, gr * 512 + gc * 128: gr * 512 + gc * 128 + 128],
                        ident[:, :])
                return last
            vtp = p.emit('tensor', tf, sem="s_tp")
            p.wait('vector', "s_tp", vtp)

            def ev(e, half=half):
                last = None
                for i in range(8):
                    blk = half * 8 + i
                    gr, gc = blk // 4, blk % 4
                    last = e.tensor_copy(
                        out=dst_bf[:, gc * 512 + gr * 128: gc * 512 + gr * 128 + 128],
                        in_=pt[:, i * 128:(i + 1) * 128])
                return last
            vev = p.emit('vector', ev, sem="s_u3v")
            hv2 = p.hop('scalar', "s_u3v", vev)
            p.guard_set(id(pt), *hv2)
        return p.cnt["s_u3v"]

    def emit_ns5():
        p.wait_cur('scalar', "s_u3v")

        def s1(s_):
            for grp in range(4):
                s_.activation(out=tsc[:, :],
                              in_=dwf[:, grp * 512:(grp + 1) * 512],
                              func=AF.Square, accum_out=tst[:, grp:grp + 1])
                s_.drain()
            return s_.activation(out=tst[:, 0:4], in_=tst[:, 0:4], func=AF.Copy)
        vs = p.emit('scalar', s1, sem="s_u3s", drain_inc=True)
        p.wait('vector', "s_u3s", vs)

        def v1(v):
            v.tensor_reduce(out=tst[:, 4:5], in_=tst[:, 0:4], axis=AX.X,
                            op=ALU.add)
            return v.drain()
        vv = p.emit('vector', v1, sem="s_u3v", drain_inc=True)
        p.wait('tensor', "s_u3v", vv)
        p.guard('tensor', id(b5))
        vm = p.emit('tensor', lambda t: t.matmul(b5[0:1, 0:1], onesf[:, 0:1],
                                                 tst[:, 4:5], start=True,
                                                 stop=True), sem="s_u3t")
        p.wait('scalar', "s_u3t", vm)

        def s2(s_):
            return s_.activation(out=tst[0:1, 5:6], in_=b5[0:1, 0:1],
                                 func=AF.Sqrt)
        vs2 = p.emit('scalar', s2, sem="s_u3s", drain_inc=True)
        hv2 = p.hop('vector', "s_u3s", vs2)
        p.guard_set(id(b5), *hv2)
        p.wait('vector', "s_u3s", vs2)

        def v2(v):
            v.tensor_scalar_add(tst[0:1, 6:7], tst[0:1, 5:6], 1e-7)
            v.drain()
            return v.reciprocal(out=tst[0:1, 7:8], in_=tst[0:1, 6:7])
        vv2 = p.emit('vector', v2, sem="s_u3v", drain_inc=True)
        p.wait('tensor', "s_u3v", vv2)
        p.guard('tensor', id(b5))
        vm2 = p.emit('tensor', lambda t: t.matmul(b5[:, 0:1], onesf[0:1, 0:128],
                                                  tst[0:1, 7:8], start=True,
                                                  stop=True), sem="s_u3t")
        p.wait('vector', "s_u3t", vm2)
        p.guard('vector', "arNS")

        def v3(v):
            v.tensor_copy(out=tst[:, 8:9], in_=b5[:, 0:1])
            v.drain()
            v.tensor_scalar(out=dwf[:, :], in0=dwf[:, :], scalar1=tst[:, 8:9],
                            scalar2=None, op0=ALU.mult)
            v.drain()
            return v.tensor_copy(out=Xb[:, :], in_=dwf[:, :])
        vv3 = p.emit('vector', v3, sem="s_u3v", drain_inc=True)
        hv2 = p.hop('scalar', "s_u3v", vv3)
        p.guard_set(id(b5), *hv2)
        p.wait('tensor', "s_u3v", vv3)
        transpose_2048(Xb, Yb)
        p.wait_cur('tensor', "s_u3v")
        for it, (ca, cb, cc2) in enumerate(NS_COEFFS):
            for mg in range(4):
                p.guard('tensor', PS4K[mg])
                acc_group(p, PS4[mg],
                          [(foldW(Yb, kg, mg), Yb[:, kg * 512:(kg + 1) * 512])
                           for kg in range(4)], sem="s_u3t")
            va = p.cnt["s_u3t"]
            p.wait('vector', "s_u3t", va)

            def evA(v):
                last = None
                for mg in range(4):
                    last = v.tensor_copy(out=Ab[:, mg * 512:(mg + 1) * 512],
                                         in_=PS4[mg])
                return last
            vA = p.emit('vector', evA, sem="s_u3v", drain_inc=True)
            p.wait('tensor', "s_u3v", vA)
            for mg in range(4):
                p.guard('tensor', PS4BK[mg])
                acc_group(p, PS4B[mg],
                          [(foldW(Ab, kg, mg), Ab[:, kg * 512:(kg + 1) * 512])
                           for kg in range(4)], sem="s_u3t")
            va2 = p.cnt["s_u3t"]
            p.wait('scalar', "s_u3t", va2)

            def evB(s_, cb=cb):
                last = None
                for mg in range(4):
                    last = s_.activation(out=bAf[:, mg * 512:(mg + 1) * 512],
                                         in_=PS4[mg], func=AF.Copy, scale=cb)
                return last
            vB1 = p.emit('scalar', evB, sem="s_u3s", drain_inc=True)
            hv2 = p.hop('vector', "s_u3s", vB1)
            for mg in range(4):
                p.guard_set(PS4K[mg], *hv2)
            p.wait('vector', "s_u3s", vB1)

            def evB2(v, cc2=cc2):
                last = None
                for mg in range(4):
                    last = v.scalar_tensor_tensor(
                        out=Bbf[:, mg * 512:(mg + 1) * 512], in0=PS4B[mg],
                        scalar=cc2, in1=bAf[:, mg * 512:(mg + 1) * 512],
                        op0=ALU.mult, op1=ALU.add)
                return last
            vB2 = p.emit('vector', evB2, sem="s_u3v", drain_inc=True)
            hv2 = p.hop('scalar', "s_u3v", vB2)
            for mg in range(4):
                p.guard_set(PS4BK[mg], *hv2)
            p.wait('tensor', "s_u3v", vB2)
            for mg in range(4):
                p.guard('tensor', PS4K[mg])
                acc_group(p, PS4[mg],
                          [(foldW(Bbf, kg, mg), Xb[:, kg * 512:(kg + 1) * 512])
                           for kg in range(4)], sem="s_u3t")
            vx = p.cnt["s_u3t"]
            p.wait('vector', "s_u3t", vx)

            def evX(v, ca=ca):
                for mg in range(4):
                    v.scalar_tensor_tensor(
                        out=dwf[:, mg * 512:(mg + 1) * 512],
                        in0=dwf[:, mg * 512:(mg + 1) * 512], scalar=ca,
                        in1=PS4[mg], op0=ALU.mult, op1=ALU.add)
                v.drain()
                return v.tensor_copy(out=Xb[:, :], in_=dwf[:, :])
            vX = p.emit('vector', evX, sem="s_u3v", drain_inc=True)
            hv2 = p.hop('scalar', "s_u3v", vX)
            for mg in range(4):
                p.guard_set(PS4K[mg], *hv2)
            if it < 4:
                p.wait('tensor', "s_u3v", vX)
                transpose_2048(Xb, Yb)
                p.wait_cur('tensor', "s_u3v")
        p.guard_set("arNS", "s_u3v", p.cnt["s_u3v"])

    for u in range(units):
        tok0 = u * SEQ
        if u > 0:
            for eng in ("sync", "gpsimd", "tensor", "vector", "scalar"):
                for sname in ("s_u3t", "s_u3v", "s_u3s", "s_tp", "s_ml",
                              "s_mo", "s_sa", "s_wl"):
                    if sname in p.cnt:
                        p.wait_cur(eng, sname)
        for t in range(32):
            lnb = sb["wgb"][t % 2]
            p.guard('sync', id(lnb))
            vl = p.dma('sync', lnb[:, :].rearrange("q (g c) -> q g c", g=KD),
                       io["lnT"][:, tok0 + t * 128: tok0 + (t + 1) * 128]
                       .rearrange("(g q) c -> q g c", q=128), sem=f"s_mx{t % 2}")
            p.wait('tensor', f"s_mx{t % 2}", vl)
            for n, ps in ((0, pw0), (1, pw1), (2, pw2)):
                p.guard('tensor', ("pw", n))
                acc_group(p, ps,
                          [(lnb[:, g * 128:(g + 1) * 128],
                            wqb[:, g * 1536 + n * 512: g * 1536 + (n + 1) * 512])
                           for g in range(KD)], sem="s_u1m")
            p.guard('tensor', id(b3))
            vem = acc_group(p, b3[:, 0:8],
                            [(lnb[:, g * 128:(g + 1) * 128],
                              exwb[:, g * 8:(g + 1) * 8]) for g in range(KD)],
                            sem="s_u1m")
            p.guard_set(id(lnb), "s_u1m", vem)
            p.wait('vector', "s_u1m", vem)
            p.guard('vector', "l0_tst")

            def vex1(v):
                return v.tensor_add(tst[:, 8:16], b3[:, 0:8], exbc[:, :])
            vxv = p.emit('vector', vex1, sem="s_u1v", drain_inc=True)
            hv2 = p.hop('scalar', "s_u1v", vxv)
            p.guard_set(id(b3), *hv2)
            p.wait('scalar', "s_u1v", vxv)

            def sex(s_, t=t):
                s_.activation(out=tst[:, 0:3], in_=tst[:, 8:11], func=AF.Abs)
                s_.drain()
                s_.activation(out=tst[:, 0:3], in_=tst[:, 0:3], func=AF.Exp,
                              scale=-1.0)
                s_.drain()
                s_.activation(out=tst[:, 0:3], in_=tst[:, 0:3], func=AF.Ln,
                              bias=1.0)
                s_.drain()
                s_.activation(out=tst[:, 3:6], in_=tst[:, 8:11], func=AF.Relu)
                s_.drain()
                s_.activation(out=extras[:, 96 + t:97 + t], in_=tst[:, 11:12],
                              func=AF.Sigmoid)
                return s_.activation(out=extras[:, 128 + t:129 + t],
                                     in_=tst[:, 12:13], func=AF.Silu)
            vse = p.emit('scalar', sex, sem="s_u1s", drain_inc=True)
            p.wait('vector', "s_u1s", vse)

            def vex2(v, t=t):
                v.tensor_add(extras[:, t:t + 1], tst[:, 0:1], tst[:, 3:4])
                v.tensor_add(extras[:, 32 + t:33 + t], tst[:, 1:2], tst[:, 4:5])
                v.tensor_add(extras[:, 64 + t:65 + t], tst[:, 2:3], tst[:, 5:6])
                return v.drain()
            vxv2 = p.emit('vector', vex2, sem="s_u1v", drain_inc=True)
            p.guard_set("l0_tst", "s_u1v", vxv2)
            p.wait('scalar', "s_u1m", p.cnt["s_u1m"])
            p.guard('scalar', "stg")

            def sraw(s_):
                s_.copy(out=stg[:, 0:512], in_=pw0)
                s_.copy(out=stg[:, 512:1024], in_=pw1)
                return s_.copy(out=stg[:, 1024:1536], in_=pw2)
            vsr = p.emit('scalar', sraw, sem="s_u1s", drain_inc=True)
            p.wait('sync', "s_u1s", vsr)
            vrd = p.dma('sync', raw_d[t * 128:(t + 1) * 128, :], stg[:, :],
                        sem="s_rd0")
            p.guard_set("stg", "s_rd0", vrd)
            for n, ps in ((0, pw0), (1, pw1), (2, pw2)):
                p.guard('scalar', "l0_tsc")
                vs1 = p.emit('scalar', lambda s_, ps=ps: s_.activation(
                    out=tsc[:, :], in_=ps, func=AF.Silu), sem="s_u1s",
                    drain_inc=True)
                if n < 2:
                    p.wait('scalar', "s_u1s", vs1)
                    p.guard('scalar', "l0_tsc2")

                    def snrm(s_):
                        s_.activation(out=tsc2[:, :], in_=tsc[:, :],
                                      func=AF.Square, accum_out=tst[:, 6:7])
                        s_.drain()
                        return s_.activation(out=tst[:, 7:8], in_=tst[:, 6:7],
                                             func=AF.Sqrt)
                    vs2 = p.emit('scalar', snrm, sem="s_u1s", drain_inc=True)
                    p.wait('vector', "s_u1s", vs2)
                    p.guard('vector', "stg2")
                    p.guard('vector', "stg3")

                    def vnrm(v, n=n, t=t):
                        v.tensor_scalar_add(tst[:, 6:7], tst[:, 7:8], 1e-5)
                        v.drain()
                        v.reciprocal(out=tst[:, 6:7], in_=tst[:, 6:7])
                        v.drain()
                        if n == 0:
                            v.tensor_scalar(out=stg2[:, 0:512], in0=tsc[:, :],
                                            scalar1=tst[:, 6:7], scalar2=None,
                                            op0=ALU.mult)
                            return v.drain()
                        v.tensor_scalar(out=tsc2[:, :], in0=tsc[:, :],
                                        scalar1=tst[:, 6:7], scalar2=None,
                                        op0=ALU.mult)
                        v.drain()
                        v.tensor_copy(out=stg2[:, 512:1024], in_=tsc2[:, :])
                        v.tensor_scalar(out=stg3[:, 0:512], in0=tsc2[:, :],
                                        scalar1=extras[:, t:t + 1],
                                        scalar2=None, op0=ALU.mult)
                        v.tensor_scalar(out=stg3[:, 512:1024], in0=tsc2[:, :],
                                        scalar1=extras[:, 64 + t:65 + t],
                                        scalar2=None, op0=ALU.mult)
                        return v.drain()
                    vv = p.emit('vector', vnrm, sem="s_u1v", drain_inc=True)
                    hv2 = p.hop('scalar', "s_u1v", vv)
                    p.guard_set(("pw", n), *hv2)
                    p.guard_set("l0_tsc", "s_u1v", vv)
                    p.guard_set("l0_tsc2", "s_u1v", vv)
                else:
                    p.wait('vector', "s_u1s", vs1)
                    p.guard('vector', "stg2")
                    p.guard('vector', "stg3")

                    def vcv(v):
                        v.tensor_copy(out=stg2[:, 1024:1536], in_=tsc[:, :])
                        v.drain()
                        return v.tensor_copy(out=stg3[:, 1024:1536],
                                             in_=tsc[:, :])
                    vv = p.emit('vector', vcv, sem="s_u1v", drain_inc=True)
                    hv2 = p.hop('scalar', "s_u1v", vv)
                    p.guard_set(("pw", n), *hv2)
                    p.guard_set("l0_tsc", "s_u1v", vv)
            vpre = p.cnt["s_u1v"]
            p.wait('tensor', "s_u1v", vpre)
            p.guard('tensor', "stgT")
            for nn in range(3):
                p.guard('tensor', id(pt))

                def tf(t_, nn=nn):
                    last = None
                    for blk in range(4):
                        last = t_.transpose(
                            pt[:, blk * 128:(blk + 1) * 128],
                            stg2[:, nn * 512 + blk * 128: nn * 512 + (blk + 1) * 128],
                            ident[:, :])
                    return last
                vtp = p.emit('tensor', tf, sem="s_tp")
                p.wait('vector', "s_tp", vtp)
                vev = p.emit('vector', lambda v, nn=nn: v.tensor_copy(
                    out=stgT[:, nn * 512:(nn + 1) * 512], in_=pt[:, 0:512]),
                    sem="s_u1v")
                hv2 = p.hop('scalar', "s_u1v", vev)
                p.guard_set(id(pt), *hv2)
            vT = p.cnt["s_u1v"]
            p.wait('sync', "s_u1v", vT)
            for (dT, off) in ((qnT_d, 0), (knT_d, 512), (vT_d, 1024)):
                p.dma('sync', dT[t].rearrange("d q c -> q d c"),
                      stgT[:, off:off + 512].rearrange("q (d c) -> q d c", d=4),
                      sem="s_rd1")
            p.dma('sync', kl0_d[t * 128:(t + 1) * 128, :], stg3[:, 0:512],
                  sem="s_rd2")
            p.dma('sync', kl2_d[t * 128:(t + 1) * 128, :], stg3[:, 512:1024],
                  sem="s_rd2")
            p.dma('sync', vt_d[t * 128:(t + 1) * 128, :], stg3[:, 1024:1536],
                  sem="s_rd2")
            p.guard_set("stgT", "s_rd1", p.cnt["s_rd1"])
            p.guard_set("stg2", "s_rd1", p.cnt["s_rd1"])
            p.guard_set("stg3", "s_rd2", p.cnt["s_rd2"])
        p.wait_cur('sync', "s_rd0")
        p.guard('sync', "arU1")
        p.guard_set("arU1", "s_rd1", p.cnt["s_rd1"])
        p.guard_set("arU1", "s_rd2", p.cnt["s_rd2"])
        for a in range(4):
            p.guard('vector', "wqfkf")
            for (col, dsto, sq) in ((a * 128, 0, True),
                                    (512 + a * 128, 8192, False)):
                dstf = arb(dsto, 4096)
                p.guard('sync', "wtok")
                p.guard('sync', "arU1")
                vld = p.dma('sync', wtok[:, :].rearrange("q (t c) -> q t c", t=32),
                            raw_d[:, col:col + 128].rearrange(
                                "(t q) c -> q t c", q=128), sem="s_wt")
                p.wait('tensor', "s_wt", vld)
                for t in range(32):
                    p.guard('tensor', id(pt))
                    vtp = p.emit('tensor', lambda t_, t=t: t_.transpose(
                        pt[:, (t % 8) * 128:(t % 8 + 1) * 128],
                        wtok[:, t * 128:(t + 1) * 128], ident[:, :]),
                        sem="s_tp")
                    p.wait('vector', "s_tp", vtp)
                    ci = a * 4 + (0 if sq else 1)
                    oi = a * 4 + (2 if sq else 3)

                    def vev_(v, t=t, dstf=dstf, ci=ci, oi=oi):
                        return v.tensor_scalar(
                            out=dstf[:, t * 128:(t + 1) * 128],
                            in0=pt[:, (t % 8) * 128:(t % 8 + 1) * 128],
                            scalar1=qksop[:, ci:ci + 1],
                            scalar2=qksop[:, oi:oi + 1],
                            op0=ALU.mult, op1=ALU.add)
                    vsv = p.emit('vector', vev_, sem="s_wtv")
                    hv2 = p.hop('scalar', "s_wtv", vsv)
                    p.guard_set(id(pt), *hv2)
                p.guard_set("wtok", "s_tp", p.cnt["s_tp"])
            p.guard('sync', "wvt")
            vvd = p.dma('sync', wvt[:, :].rearrange("q (t c) -> q t c", t=32),
                        raw_d[:, 1024 + a * 128:1024 + (a + 1) * 128].rearrange(
                            "(t q) c -> q t c", q=128), sem="s_wv")
            p.wait('tensor', "s_wv", vvd)
            p.wait_cur('tensor', "s_wtv")
            for qt in range(32):
                q0 = qt * 128
                kbase = max(0, q0 - 1024)
                span = q0 + 128 - kbase
                nblk = span // 128
                for n in range(3):
                    p.guard('tensor', ("pw", n))
                off = 0
                while off < span:
                    w = min(512, span - off)
                    p.emit('tensor', lambda t_, qt=qt, off=off, w=w, kbase=kbase:
                           t_.matmul(pw[:, off:off + w],
                                     wqf[:, qt * 128:(qt + 1) * 128],
                                     wkf[:, kbase + off: kbase + off + w],
                                     start=True, stop=True), sem="s_ws")
                    off += w
                vsm = p.cnt["s_ws"]
                p.wait('scalar', "s_ws", vsm)
                p.guard('scalar', "wsc")
                vsv = p.emit('scalar', lambda s_, span=span: s_.copy(
                    out=wsc[:, 0:span], in_=pw[:, 0:span]), sem="s_wse",
                    drain_inc=True)
                hv2 = p.hop('vector', "s_wse", vsv)
                for n in range(3):
                    p.guard_set(("pw", n), *hv2)
                p.wait('vector', "s_wse", vsv)
                p.guard('vector', "wstat")
                if qt < 8:
                    msl = WM1[:, 1024 - q0: 1024 - q0 + span]
                else:
                    msl = WM3[:, 0:span]

                def vvm_fn(v, span=span, msl=msl):
                    v.tensor_add(wsc[:, 0:span], wsc[:, 0:span], msl)
                    v.drain()
                    return v.reduce_max(out=wst[:, 0:1], in_=wsc[:, 0:span],
                                        axis=AX.X, negate=True)
                vvm = p.emit('vector', vvm_fn, sem="s_wsv", drain_inc=True)
                p.wait('scalar', "s_wsv", vvm)
                p.guard('scalar', "wP")
                vse2 = p.emit('scalar', lambda s_, span=span: s_.activation(
                    out=wP[:, 0:span], in_=wsc[:, 0:span], func=AF.Exp,
                    bias=wst[:, 0:1], accum_out=wst[:, 1:2]), sem="s_wse",
                    drain_inc=True)
                p.guard_set("wsc", "s_wse", vse2)
                p.wait('vector', "s_wse", vse2)
                vrc = p.emit('vector', lambda v: v.reciprocal(
                    out=wst[:, 2:3], in_=wst[:, 1:2]), sem="s_wsv",
                    drain_inc=True)
                p.guard_set("wstat", "s_wsv", vrc)
                p.wait('tensor', "s_wse", vse2)
                for blk in range(nblk):
                    p.guard('tensor', id(pt))
                    vtp = p.emit('tensor', lambda t_, blk=blk: t_.transpose(
                        pt[:, (blk % 8) * 128:(blk % 8 + 1) * 128],
                        wP[:, blk * 128:(blk + 1) * 128], ident[:, :]),
                        sem="s_tp")
                    p.wait('vector', "s_tp", vtp)
                    vpc = p.emit('vector', lambda v, blk=blk: v.tensor_copy(
                        out=wPT[:, blk * 128:(blk + 1) * 128],
                        in_=pt[:, (blk % 8) * 128:(blk % 8 + 1) * 128]),
                        sem="s_wsv")
                    hv2 = p.hop('scalar', "s_wsv", vpc)
                    p.guard_set(id(pt), *hv2)
                p.guard_set("wP", "s_tp", p.cnt["s_tp"])
                p.wait_cur('tensor', "s_wsv")
                p.guard('tensor', id(b3))

                def pv(t_, nblk=nblk, kbase=kbase):
                    last = None
                    for blk in range(nblk):
                        kt = kbase // 128 + blk
                        last = t_.matmul(b3[:, 0:128],
                                         wPT[:, blk * 128:(blk + 1) * 128],
                                         wvt[:, kt * 128:(kt + 1) * 128],
                                         start=(blk == 0),
                                         stop=(blk == nblk - 1))
                    return last
                vpv = p.emit('tensor', pv, sem="s_wpv")
                p.wait('scalar', "s_wpv", vpv)
                p.wait('scalar', "s_wsv", vrc)
                p.guard('scalar', "wob")
                veo = p.emit('scalar', lambda s_: s_.activation(
                    out=wob[:, :], in_=b3[:, 0:128], func=AF.Copy,
                    scale=wst[:, 2:3]), sem="s_wse", drain_inc=True)
                hv2 = p.hop('vector', "s_wse", veo)
                p.guard_set(id(b3), *hv2)
                p.wait('sync', "s_wse", veo)
                vwd = p.dma('sync', win_d[q0:q0 + 128, a * 128:(a + 1) * 128],
                            wob[:, :], sem="s_rd3")
                p.guard_set("wob", "s_rd3", vwd)
            p.guard_set("wqfkf", "s_ws", p.cnt["s_ws"])
            p.guard_set("wvt", "s_wpv", p.cnt["s_wpv"])
        for sname in ("s_wpv", "s_tp", "s_rd3", "s_ws", "s_wse", "s_wsv",
                      "s_wtv"):
            p.guard_set("arU2", sname, p.cnt.get(sname, 0))
        p.guard('sync', "arU2")
        p.guard('sync', "l0_Wm")
        for i, Wm in enumerate(WN):
            p.dma('sync', Wm[:, :].rearrange("q (g c) -> q g c", g=4),
                  io["w_init"][i].rearrange("(g q) c -> q g c", q=128),
                  sem="s_wi")
        for i, WT in enumerate(WTs):
            p.dma('gpsimd', WT[:, :].rearrange("q (g c) -> q g c", g=4),
                  io["wT_init"][i].rearrange("(g q) c -> q g c", q=128),
                  sem="s_wi2")
        vwi = p.cnt["s_wi"]
        vwi2 = p.cnt["s_wi2"]
        p.guard('gpsimd', "Mzero")
        p.guard('gpsimd', "arU2")
        p.guard('gpsimd', "Mbf")
        vzm = p.emit('gpsimd', lambda g: (g.memset(Mbf[:, :], 0.0),
                                          g.drain())[-1], sem="s_wg2",
                     drain_inc=True)
        p.wait('gpsimd', "s_wg2", vzm)
        for i in range(3):
            p.dma('gpsimd', m_d[i], Mbf[:, :], sem="s_mz")
        vzd = p.cnt["s_mz"]
        p.guard_set("Mzero", "s_mz", vzd)
        p.guard_set("Mbf", "s_mz", vzd)
        p.wait('vector', "s_wi", vwi)
        p.wait('vector', "s_wi2", vwi2)
        p.guard('vector', "W1N")
        vwn1 = p.emit('vector', lambda v: v.tensor_copy(out=W1N[:, :],
                                                        in_=W1[:, :]),
                      sem="s_u3v", drain_inc=True)
        p.wait('scalar', "s_wi", vwi)
        p.guard('scalar', "Wn")

        def snrm0(s_):
            for i, Wm in enumerate(WN):
                for grp in range(4):
                    s_.activation(out=tsc[:, :],
                                  in_=Wm[:, grp * 512:(grp + 1) * 512],
                                  func=AF.Square,
                                  accum_out=Wn[:, i * 4 + grp:i * 4 + grp + 1])
                    s_.drain()
            return s_.activation(out=Wn[:, 0:12], in_=Wn[:, 0:12],
                                 func=AF.Sqrt)
        vsn = p.emit('scalar', snrm0, sem="s_u3s", drain_inc=True)
        p.guard_set("Wn", "s_u3s", vsn)
        p.wait('tensor', "s_u3v", vwn1)
        p.wait('tensor', "s_wi2", vwi2)
        p.wait('tensor', "s_wi", vwi)
        p.wait_cur('vector', "s_u1s")
        p.wait_cur('tensor', "s_u1s")
        p.wait_cur('tensor', "s_u1v")

        for cc in range(NCH):
            for hc in range(2):
                ht0 = cc * 8 + hc * 4
                for (dT, dst, key) in ((qnT_d, sA, "sA"), (knT_d, sB, "sB"),
                                       (vT_d, sC, "sC")):
                    p.guard('sync', key)
                    p.guard('sync', "arNS")
                    for d in range(4):
                        p.dma('sync',
                              dst[:, d * 512:(d + 1) * 512].rearrange(
                                  "q (t c) -> q t c", t=4),
                              dT[ht0:ht0 + 4, d].rearrange("t q c -> q t c"),
                              sem="s_sa")
                vsa = p.cnt["s_sa"]
                p.wait('tensor', "s_sa", vsa)

                def sfold(s, kg):
                    return s[:, kg * 512:(kg + 1) * 512]
                for mg in range(4):
                    sl = slice(mg * 512, (mg + 1) * 512)
                    p.guard('tensor', PS4K[mg])
                    acc_group(p, PS4[mg],
                              [(foldW(W2T, kg, mg), sfold(sA, kg))
                               for kg in range(4)], sem="s_u3t")
                    p.guard('tensor', PS4BK[mg])
                    vm_ = acc_group(p, PS4B[mg],
                                    [(foldW(W0T, kg, mg), sfold(sA, kg))
                                     for kg in range(4)], sem="s_u3t")
                    p.wait('scalar', "s_u3t", vm_)
                    p.guard('scalar', "och")
                    vsg = p.emit('scalar', lambda s_, mg=mg: s_.activation(
                        out=och[:, :], in_=PS4B[mg], func=AF.Silu),
                        sem="s_u3s", drain_inc=True)
                    p.wait('vector', "s_u3s", vsg)
                    p.guard('vector', "ghb")
                    vgh_ = p.emit('vector', lambda v, mg=mg, sl=sl:
                                  v.tensor_mul(ghb[:, sl], och[:, :],
                                               PS4[mg]), sem="s_u3v")
                    hv2 = p.hop('scalar', "s_u3v", vgh_)
                    p.guard_set(PS4K[mg], *hv2)
                    p.guard_set(PS4BK[mg], *hv2)
                    p.guard_set("och", "s_u3v", vgh_)
                specs = [(gba, W0T, sB, "gba2", "gba"),
                         (hbmb, W2T, sB, None, "hbmb"),
                         (dhb, W1N, sC, None, "dhb")]
                for (dst, WT, srcS, act, dkey) in specs:
                    for mg in range(4):
                        pp = PS4[mg]
                        ppk = PS4K[mg]
                        p.guard('tensor', ppk)
                        vm_ = acc_group(
                            p, pp,
                            [(foldW(WT, kg, mg), sfold(srcS, kg))
                             for kg in range(4)], sem="s_u3t")
                        p.wait('scalar', "s_u3t", vm_)
                        p.guard('scalar', dkey)
                        sl = slice(mg * 512, (mg + 1) * 512)
                        if act == "gba2":
                            def sev2(s_, pp=pp, sl=sl):
                                s_.copy(out=gba[:, sl], in_=pp)
                                s_.drain()
                                return s_.activation(out=sgb[:, sl], in_=pp,
                                                     func=AF.Silu)
                            vse_ = p.emit('scalar', sev2, sem="s_u3s",
                                          drain_inc=True)
                        else:
                            vse_ = p.emit('scalar', lambda s_, pp=pp, dst=dst,
                                          sl=sl, act=act: s_.activation(
                                              out=dst[:, sl], in_=pp,
                                              func=(act or AF.Copy)),
                                          sem="s_u3s", drain_inc=True)
                        hv2 = p.hop('vector', "s_u3s", vse_)
                        p.guard_set(ppk, *hv2)
                vghv = p.cnt["s_u3v"]
                p.wait('tensor', "s_u3v", vghv)
                for tt in range(4):
                    tglob = ht0 + tt
                    pp = PS4[tt % 4]
                    ppk = PS4K[tt % 4]
                    p.guard('tensor', ppk)
                    vm_ = acc_group(
                        p, pp,
                        [(ghb[:, kg * 512 + tt * 128: kg * 512 + tt * 128 + 128],
                          W1T[:, kg * 512:(kg + 1) * 512]) for kg in range(4)],
                        sem="s_u3t")
                    p.wait('scalar', "s_u3t", vm_)
                    p.guard('scalar', "och")

                    def sot(s_, pp=pp):
                        s_.activation(out=och[:, :], in_=pp, func=AF.Copy)
                        s_.drain()
                        s_.activation(out=tsc[:, :], in_=och[:, :],
                                      func=AF.Square, accum_out=tst[:, 6:7])
                        s_.drain()
                        return s_.activation(out=tst[:, 7:8], in_=tst[:, 6:7],
                                             func=AF.Sqrt,
                                             scale=float(1.0 / 512), bias=1e-5)
                    vso = p.emit('scalar', sot, sem="s_u3s", drain_inc=True)
                    hv2 = p.hop('vector', "s_u3s", vso)
                    p.guard_set(ppk, *hv2)
                    p.guard('sync', "tsc2w")
                    vwl = p.dma('sync', tsc2[:, :],
                                win_d[tglob * 128:(tglob + 1) * 128, :],
                                sem="s_wl")
                    p.wait('vector', "s_wl", vwl)
                    p.wait('vector', "s_u3s", vso)
                    p.guard('vector', "och")

                    def vot(v, tglob=tglob):
                        v.reciprocal(out=tst[:, 6:7], in_=tst[:, 7:8])
                        v.drain()
                        v.tensor_scalar(out=och[:, :], in0=och[:, :],
                                        scalar1=tst[:, 6:7],
                                        scalar2=extras[:, 128 + tglob:
                                                       129 + tglob],
                                        op0=ALU.mult, op1=ALU.mult)
                        v.drain()
                        v.tensor_mul(och[:, :], och[:, :], ongbc[:, :])
                        v.drain()
                        v.tensor_add(och[:, :], och[:, :], tsc2[:, :])
                        v.drain()
                        return v.tensor_copy(out=tmpb[:, 0:512],
                                             in_=och[:, :])
                    vvo = p.emit('vector', vot, sem="s_u3v", drain_inc=True)
                    p.guard_set("tsc2w", "s_u3v", vvo)
                    p.guard_set("och", "s_u3v", vvo)
                    p.wait('tensor', "s_u3v", vvo)
                    p.guard('tensor', id(pt))
                    p.guard('tensor', "tmpbT")

                    def tot(t_):
                        last = None
                        for blk in range(4):
                            last = t_.transpose(
                                pt[:, blk * 128:(blk + 1) * 128],
                                tmpb[:, blk * 128:(blk + 1) * 128],
                                ident[:, :])
                        return last
                    vto = p.emit('tensor', tot, sem="s_tp")
                    p.wait('scalar', "s_tp", vto)
                    vos = p.emit('scalar', lambda s_: s_.copy(
                        out=tsc2[:, :].bitcast(BF16)[:, 0:512],
                        in_=pt[:, 0:512]), sem="s_u3s", drain_inc=True)
                    hv2 = p.hop('vector', "s_u3s", vos)
                    p.guard_set(id(pt), *hv2)
                    p.guard_set("tmpbT", "s_u3s", vos)
                    p.wait('sync', "s_u3s", vos)
                    vod = p.dma('sync',
                                io["mixT"][:, tok0 + tglob * 128:
                                           tok0 + (tglob + 1) * 128]
                                .rearrange("(d q) c -> q d c", q=128),
                                tsc2[:, :].bitcast(BF16)[:, 0:512]
                                .rearrange("q (d c) -> q d c", d=4),
                                sem="s_mo")
                    p.guard_set("tsc2w", "s_mo", vod)
                p.wait_cur('vector', "s_u3s")
                p.guard('vector', "hidb")
                p.guard('vector', "Mbf")

                def velem(v):
                    v.drain()
                    v.tensor_mul(hidb[:, :], sgb[:, :], hbmb[:, :])
                    v.drain()
                    v.tensor_mul(tmp[:, :], dhb[:, :], hbmb[:, :])
                    return v.drain()
                vve = p.emit('vector', velem, sem="s_u3v", drain_inc=True)
                p.wait('scalar', "s_u3v", vve)
                for mg in range(4):
                    sl = slice(mg * 512, (mg + 1) * 512)
                    p.guard('scalar', "l0_tsc")
                    vsg2 = p.emit('scalar', lambda s_, sl=sl: s_.activation(
                        out=tsc[:, :], in_=gba[:, sl], func=AF.Sigmoid),
                        sem="s_u3s", drain_inc=True)
                    p.wait('vector', "s_u3s", vsg2)
                    p.guard('vector', "och")

                    def vdg(v, sl=sl):
                        v.tensor_scalar(out=och[:, :], in0=tsc[:, :],
                                        scalar1=-1.0, scalar2=1.0,
                                        op0=ALU.mult, op1=ALU.add)
                        v.drain()
                        v.tensor_mul(och[:, :], och[:, :], gba[:, sl])
                        v.drain()
                        v.tensor_scalar_add(och[:, :], och[:, :], 1.0)
                        v.drain()
                        v.tensor_mul(och[:, :], och[:, :], tmp[:, sl])
                        v.drain()
                        v.tensor_mul(och[:, :], och[:, :], tsc[:, :])
                        v.drain()
                        return v.tensor_copy(out=tmpb[:, sl], in_=och[:, :])
                    vdgv = p.emit('vector', vdg, sem="s_u3v", drain_inc=True)
                    p.guard_set("l0_tsc", "s_u3v", vdgv)
                    p.guard_set("och", "s_u3v", vdgv)
                    p.guard_set("gba", "s_u3v", vdgv)

                def velem3(v):
                    v.drain()
                    v.tensor_mul(dhb[:, :], dhb[:, :], sgb[:, :])
                    return v.drain()
                vv3 = p.emit('vector', velem3, sem="s_u3v", drain_inc=True)
                p.guard_set("Mbf", "s_u3v", vv3)
                p.wait('tensor', "s_u3v", vv3)
                for (srcb, dstb, dkey) in ((tmpb, ghb, "ghb"),
                                           (dhb, sgb, "sgb"),
                                           (hidb, hbmb, "hbmb")):
                    p.guard('tensor', dkey + "T")
                    for half in range(2):
                        p.guard('tensor', id(pt))

                        def tf2(t_, srcb=srcb, half=half):
                            last = None
                            for i in range(8):
                                blk = half * 8 + i
                                gr, gc = blk // 4, blk % 4
                                last = t_.transpose(
                                    pt[:, i * 128:(i + 1) * 128],
                                    srcb[:, gr * 512 + gc * 128:
                                         gr * 512 + (gc + 1) * 128],
                                    ident[:, :])
                            return last
                        vtp = p.emit('tensor', tf2, sem="s_tp")
                        p.wait('vector', "s_tp", vtp)

                        def ev2(v, dstb=dstb, half=half):
                            last = None
                            for i in range(8):
                                blk = half * 8 + i
                                gr, gc = blk // 4, blk % 4
                                last = v.tensor_copy(
                                    out=dstb[:, gc * 512 + gr * 128:
                                             gc * 512 + (gr + 1) * 128],
                                    in_=pt[:, i * 128:(i + 1) * 128])
                            return last
                        vev = p.emit('vector', ev2, sem="s_u3v")
                        hv2 = p.hop('scalar', "s_u3v", vev)
                        p.guard_set(id(pt), *hv2)
                vtok = p.cnt["s_u3v"]
                p.wait('vector', "s_u3v", vtok)
                def vhl1(v, ht0=ht0):
                    v.drain()
                    for tt in range(4):
                        tglob = ht0 + tt
                        v.tensor_scalar(out=hbmb[:, tt * 512:(tt + 1) * 512],
                                        in0=hbmb[:, tt * 512:(tt + 1) * 512],
                                        scalar1=extras[:, 32 + tglob:
                                                       33 + tglob],
                                        scalar2=None, op0=ALU.mult)
                    return v.drain()
                vhlv = p.emit('vector', vhl1, sem="s_u3v", drain_inc=True)
                p.guard_set("sA", "s_u3t", p.cnt["s_u3t"])
                p.guard_set("sB", "s_u3t", p.cnt["s_u3t"])
                p.guard_set("sC", "s_u3t", p.cnt["s_u3t"])
                for (dS, dst, key) in ((kl0_d, sA, "sA"), (kl2_d, sB, "sB"),
                                       (vt_d, sC, "sC")):
                    p.guard('sync', key)
                    p.dma('sync', dst.rearrange("q (t c) -> q t c", t=4),
                          dS[ht0 * 128:(ht0 + 4) * 128].rearrange(
                              "(t q) c -> q t c", q=128), sem="s_sa")
                vsb = p.cnt["s_sa"]
                p.wait('tensor', "s_sa", vsb)
                p.wait('tensor', "s_u3v", vhlv)
                for (iw, lhsb, rhsb) in ((0, ghb, sA), (1, sC, hbmb),
                                         (2, sgb, sB)):
                    for mg in range(4):
                        pp = PS4[mg]
                        ppk = PS4K[mg]
                        p.guard('tensor', ppk)
                        vm_ = acc_group(
                            p, pp,
                            [(lhsb[:, kc * 512 + mg * 128:
                                   kc * 512 + mg * 128 + 128],
                              rhsb[:, kc * 512:(kc + 1) * 512])
                             for kc in range(4)], sem="s_u3t")
                        p.wait('vector', "s_u3t", vm_)
                        sl = slice(mg * 512, (mg + 1) * 512)
                        p.guard('vector', ("dwa", iw))
                        if hc == 0:
                            fn = (lambda v, pp=pp, sl=sl, iw=iw:
                                  v.tensor_copy(out=dwa[iw][:, sl], in_=pp))
                        else:
                            fn = (lambda v, pp=pp, sl=sl, iw=iw:
                                  v.tensor_add(dwa[iw][:, sl],
                                               dwa[iw][:, sl], pp))
                        vda = p.emit('vector', fn, sem="s_u3v")
                        hv2 = p.hop('scalar', "s_u3v", vda)
                        p.guard_set(ppk, *hv2)
                    p.guard_set(("dwa", iw), "s_u3v", p.cnt["s_u3v"])
                for k in ("ghb", "sgb", "hbmb", "sA", "sB", "sC",
                          "gba", "dhb", "hidb"):
                    p.guard_set(k, "s_u3t", p.cnt["s_u3t"])
                p.guard_set("ghbT", "s_u3t", p.cnt["s_u3t"])
                p.guard_set("sgbT", "s_u3t", p.cnt["s_u3t"])
                p.guard_set("hbmbT", "s_u3t", p.cnt["s_u3t"])
            p.guard('tensor', id(b4))
            vmm1 = p.emit('tensor', lambda t_, cc=cc: t_.matmul(
                b4[0:1, 0:8], onesf[:, 0:1],
                extras[:, 96 + cc * 8:96 + (cc + 1) * 8],
                start=True, stop=True), sem="s_u3t")
            p.wait('vector', "s_u3t", vmm1)
            vmr = p.emit('vector', lambda v: (v.tensor_reduce(
                out=tst[0:1, 9:10], in_=b4[0:1, 0:8], axis=AX.X, op=ALU.add),
                v.drain())[-1], sem="s_u3v", drain_inc=True)
            hv2 = p.hop('scalar', "s_u3v", vmr)
            p.guard_set(id(b4), *hv2)
            p.wait('tensor', "s_u3v", vmr)
            p.guard('tensor', id(b4))
            vmm2 = p.emit('tensor', lambda t_: t_.matmul(
                b4[:, 0:1], onesf[0:1, 0:128], tst[0:1, 9:10],
                start=True, stop=True), sem="s_u3t")
            p.wait('scalar', "s_u3t", vmm2)
            p.guard('scalar', "mmv")
            vmv = p.emit('scalar', lambda s_: s_.activation(
                out=mmv[:, 0:1], in_=b4[:, 0:1], func=AF.Copy,
                scale=float(1.0 / CH)), sem="s_u3s", drain_inc=True)
            hv2 = p.hop('vector', "s_u3s", vmv)
            p.guard_set(id(b4), *hv2)
            p.wait('vector', "s_u3s", vmv)
            for iw in range(3):
                p.guard('sync', "Mbf")
                vml = p.dma('sync', Mbf[:, :], m_d[iw], sem="s_ml")
                p.wait('vector', "s_ml", vml)
                p.guard('vector', "dwf")
                p.guard('vector', ("dwa", iw))

                def vdw(v, iw=iw):
                    v.drain()
                    v.scalar_tensor_tensor(out=dwf[:, :], in0=Mbf[:, :],
                                           scalar=mmv[:, 0:1],
                                           in1=dwa[iw][:, :],
                                           op0=ALU.mult, op1=ALU.add)
                    v.drain()
                    return v.tensor_copy(out=Mbf[:, :], in_=dwf[:, :])
                vdv = p.emit('vector', vdw, sem="s_u3v", drain_inc=True)
                p.guard_set("mmv", "s_u3v", vdv)
                p.wait('sync', "s_u3v", vdv)
                vms = p.dma('sync', m_d[iw], Mbf[:, :], sem="s_ml")
                p.guard_set("Mbf", "s_ml", vms)
                dbg_dump("dwm", cc, iw, dwf[:, :])
                emit_ns5()
                dbg_dump("dwn", cc, iw, dwf[:, :])
                p.wait_cur('vector', "s_u3v")
                p.guard('vector', ("Wupd", iw))

                def vwadd(v, iw=iw):
                    v.drain()
                    return v.tensor_add(WN[iw][:, :], WN[iw][:, :], dwf[:, :])
                vwa = p.emit('vector', vwadd, sem="s_u3v", drain_inc=True)
                p.guard_set("dwf", "s_u3v", vwa)
                p.wait('scalar', "s_u3v", vwa)

                def swn(s_, iw=iw):
                    for grp in range(4):
                        s_.activation(out=tsc[:, :],
                                      in_=WN[iw][:, grp * 512:(grp + 1) * 512],
                                      func=AF.Square,
                                      accum_out=tst[:, 10 + grp:11 + grp])
                        s_.drain()
                    return s_.activation(out=tst[:, 10:14], in_=tst[:, 10:14],
                                         func=AF.Sqrt)
                vsw = p.emit('scalar', swn, sem="s_u3s", drain_inc=True)
                p.wait('vector', "s_u3s", vsw)
                p.guard('vector', "arNS")

                def vrn(v, iw=iw):
                    v.tensor_scalar_add(tst[:, 10:14], tst[:, 10:14], 1e-5)
                    v.drain()
                    v.reciprocal(out=tst[:, 10:14], in_=tst[:, 10:14])
                    v.drain()
                    v.tensor_mul(tst[:, 10:14], tst[:, 10:14],
                                 Wn[:, iw * 4:iw * 4 + 4])
                    v.drain()
                    for grp in range(4):
                        v.tensor_scalar(
                            out=WN[iw][:, grp * 512:(grp + 1) * 512],
                            in0=WN[iw][:, grp * 512:(grp + 1) * 512],
                            scalar1=tst[:, 10 + grp:11 + grp], scalar2=None,
                            op0=ALU.mult)
                    v.drain()
                    return v.tensor_copy(out=Xb[:, :], in_=WN[iw][:, :])
                vrnv = p.emit('vector', vrn, sem="s_u3v", drain_inc=True)
                p.guard_set(("Wupd", iw), "s_u3v", vrnv)
                dbg_dump("W", cc, iw, WN[iw][:, :])
                p.wait('tensor', "s_u3v", vrnv)
                transpose_2048(Xb, WTs[iw])
                if iw == 1:
                    p.guard('vector', "W1N")
                    vw1n = p.emit('vector', lambda v: (v.drain(),
                                                       v.tensor_copy(
                        out=W1N[:, :], in_=Xb[:, :]))[-1], sem="s_u3v",
                        drain_inc=True)
                p.guard_set("arNS", "s_u3v", p.cnt["s_u3v"])
                p.wait_cur('tensor', "s_u3v")
            p.guard_set("W1N", "s_u3t", p.cnt["s_u3t"])
        p.guard_set("l0_Wm", "s_u3t", p.cnt["s_u3t"])
        p.guard_set("arU1", "s_u3t", p.cnt["s_u3t"])
        p.guard_set("arU1", "s_u3v", p.cnt["s_u3v"])
        p.guard_set("wtok", "s_tp", p.cnt["s_tp"])
        p.guard_set("wvt", "s_wpv", p.cnt["s_wpv"])
        p.guard_set("l0_wqb", "s_u1m", p.cnt["s_u1m"])
        p.guard_set("l0_tst", "s_u3s", p.cnt["s_u3s"])
        p.guard_set("l0_tst", "s_u3v", p.cnt["s_u3v"])


def build_launch2(units=2, debug=False):
    nc = bass.Bass("TRN2", target_bir_lowering=False)
    io = {}
    io["lnT"] = nc.dram_tensor("lnT", [D, 2 * SEQ], BF16, kind="ExternalInput").ap()
    io["wq_cols"] = nc.dram_tensor("wq_cols", [D, 1536], F32, kind="ExternalInput").ap()
    io["ex_w"] = nc.dram_tensor("ex_w", [D, 8], F32, kind="ExternalInput").ap()
    io["ex_b"] = nc.dram_tensor("ex_b", [1, 8], F32, kind="ExternalInput").ap()
    io["w_init"] = nc.dram_tensor("w_init", [3, 512, 512], F32, kind="ExternalInput").ap()
    io["wT_init"] = nc.dram_tensor("wT_init", [3, 512, 512], F32, kind="ExternalInput").ap()
    io["qk_so"] = nc.dram_tensor("qk_so", [512, 4], F32, kind="ExternalInput").ap()
    io["onorm_g"] = nc.dram_tensor("onorm_g", [1, 512], F32, kind="ExternalInput").ap()
    io["mixT"] = nc.dram_tensor("mixT", [512, 2 * SEQ], BF16, kind="ExternalOutput").ap()
    dbg = None
    if debug:
        dbg = {k: nc.dram_tensor(f"dbg_{k}", [NCH, 3, 128, 2048], F32,
                                 kind="ExternalOutput").ap()
               for k in ("dwm", "dwn", "W")}
    with ExitStack() as stack:
        p = Prog(nc, stack)
        p.const(1e-5)
        p.const(1e-7)
        p.const(1.0)
        sb = {}
        build_l0attn(p, io, sb, units=units, dbg=dbg)
        p.wait_cur('scalar', "s_mo")
        p.replay()
    return nc


def build_launch1(debug=False):
    nc = bass.Bass("TRN2", target_bir_lowering=False)
    xin = nc.dram_tensor("xin", [NTOK, D], F32, kind="ExternalInput").ap()
    x1 = nc.dram_tensor("x1", [NTOK, D], F32, kind="ExternalOutput").ap()
    ln2T = nc.dram_tensor("ln2T", [D, NTOK], BF16, kind="ExternalOutput").ap()
    dbg = None
    if debug:
        dbg = {
            "acc": nc.dram_tensor("dbg_acc", [NT, D], F32, kind="ExternalOutput").ap(),
            "comb": nc.dram_tensor("dbg_comb", [NT, 8], F32, kind="ExternalOutput").ap(),
            "comb2": nc.dram_tensor("dbg_comb2", [NT, 8], F32, kind="ExternalOutput").ap(),
            "lg": nc.dram_tensor("dbg_lg", [NT, 8], F32, kind="ExternalOutput").ap(),
            "xlnT": nc.dram_tensor("dbg_xlnT", [D, NT], BF16, kind="ExternalOutput").ap(),
        }
    w = moe_weight_aps(nc)
    with ExitStack() as stack:
        p = Prog(nc, stack)
        p.const(1e-5)
        sb = {}
        build_moe_sublayer(p, "m", xin, x1, ln2T, w, sb, dbg=dbg)
        p.wait_cur('scalar', "s_st0")
        p.wait_cur('scalar', "s_st1")
        p.wait_cur('scalar', "s_lnst")
        p.replay()
    return nc


def _l0mixer_core_inputs(inputs, lnT_full, core):
    h, bp = core % 4, core // 4
    lnT = lnT_full[bp]
    wqkv = np.asarray(inputs["L0_attn_Wqkv"], np.float32)
    wq_cols = np.ascontiguousarray(np.concatenate(
        [wqkv[:, part * D + h * 512: part * D + (h + 1) * 512]
         for part in range(3)], 1))
    ex_w = np.concatenate([
        np.asarray(inputs["L0_attn_Wlr"], np.float32)[:, 3 * h:3 * h + 3],
        np.asarray(inputs["L0_attn_Wm"], np.float32)[:, h:h + 1],
        np.asarray(inputs["L0_attn_Ws"], np.float32)[:, h:h + 1],
        np.zeros((D, 3), np.float32)], 1)
    ex_b = np.array([[BASE_LR_INV] * 3 +
                     [float(np.asarray(inputs["L0_attn_bm"])[h]),
                      float(np.asarray(inputs["L0_attn_bs"])[h]),
                      0.0, 0.0, 0.0]], np.float32)
    w_init = np.stack([np.asarray(inputs[f"L0_attn_w{i}"], np.float32)[h]
                       for i in range(3)])
    wT_init = np.ascontiguousarray(w_init.transpose(0, 2, 1))
    qs = np.asarray(inputs["L0_attn_qk_scale"], np.float32)[h * 512:(h + 1) * 512]
    qo = np.asarray(inputs["L0_attn_qk_offset"], np.float32)[h * 512:(h + 1) * 512]
    qk_so = np.ascontiguousarray(
        np.stack([qs[:, 0], qs[:, 1], qo[:, 0], qo[:, 1]], 1))
    onorm_g = np.asarray(inputs["L0_attn_o_norm_g"],
                         np.float32).reshape(1, 512)
    return {"lnT": lnT, "wq_cols": wq_cols, "ex_w": ex_w, "ex_b": ex_b,
            "w_init": np.ascontiguousarray(w_init), "wT_init": wT_init,
            "qk_so": qk_so, "onorm_g": onorm_g}


def _l1attn_core_inputs(inputs, lnT_full, core):
    g, bp = core % 4, core // 4
    wqkv = np.asarray(inputs["L1_attn_w_qkv"], np.float32)
    wq_cols = np.ascontiguousarray(np.concatenate(
        [wqkv[:, part * D + g * 512: part * D + (g + 1) * 512]
         for part in range(3)], 1))
    R = np.zeros((128, 128), np.float32)
    for i in range(0, 128, 2):
        R[i + 1, i] = -1.0
        R[i, i + 1] = 1.0
    return {"lnT": lnT_full[bp], "wq_cols": wq_cols, "ropeRT": R}


def _assemble_mixT(results, core):
    bp, j = core // 4, core % 4
    return np.concatenate(
        [results[bp * 4 + h]["mixT" if "mixT" in results[bp * 4 + h]
                             else "a1T"][:, j * NTOK:(j + 1) * NTOK]
         for h in range(4)], 0)


def kernel(**inputs):
    x = np.asarray(inputs["L0_attn_x"], np.float32).reshape(8 * NTOK, D)
    warr = moe_weight_arrays(inputs)
    qarr = l1ffn_weight_arrays(inputs)
    cores = list(range(NCORES))

    nc1 = build_launch1()
    maps = []
    for c in cores:
        m = {"xin": np.ascontiguousarray(x[c * NTOK:(c + 1) * NTOK])}
        m.update(warr)
        maps.append(m)
    r1 = run_bass_kernel_spmd(nc1, maps, core_ids=cores)
    x1 = [r1.results[c]["x1"] for c in cores]
    ln2T_full = [np.concatenate([r1.results[4 * bp + j]["ln2T"]
                                 for j in range(4)], 1) for bp in range(2)]

    nc2 = build_launch2()
    maps = [_l0mixer_core_inputs(inputs, ln2T_full, c) for c in cores]
    r2 = run_bass_kernel_spmd(nc2, maps, core_ids=cores)

    nc3 = build_launch3()
    maps = []
    for c in cores:
        m = {"x1": x1[c], "mixT": _assemble_mixT(r2.results, c),
             "Wo": np.asarray(inputs["L0_attn_Wo"], np.float32)}
        m.update(warr)
        m.update(qarr)
        maps.append(m)
    r3 = run_bass_kernel_spmd(nc3, maps, core_ids=cores)
    x4 = [r3.results[c]["x4"] for c in cores]
    ln5T_full = [np.concatenate([r3.results[4 * bp + j]["ln5T"]
                                 for j in range(4)], 1) for bp in range(2)]

    nc4 = build_launch4()
    maps = [_l1attn_core_inputs(inputs, ln5T_full, c) for c in cores]
    r4 = run_bass_kernel_spmd(nc4, maps, core_ids=cores)

    nc5 = build_launch5()
    maps = []
    for c in cores:
        m = {"x4": x4[c], "a1T": _assemble_mixT(r4.results, c),
             "Wp": np.asarray(inputs["L1_attn_w_proj"], np.float32)}
        m.update(qarr)
        maps.append(m)
    r5 = run_bass_kernel_spmd(nc5, maps, core_ids=cores)
    out = np.concatenate([r5.results[c]["x6"] for c in cores], 0)
    return np.ascontiguousarray(out.reshape(4, LB, D).astype(np.float32))
